# version 18
# baseline (speedup 1.0000x reference)
import numpy as np
import ml_dtypes
import concourse.bass as bass
import concourse.mybir as mybir
from concourse.bass import IndirectOffsetOnAxis
from contextlib import ExitStack

F32 = mybir.dt.float32
BF16 = mybir.dt.bfloat16
I32 = mybir.dt.int32
U32 = mybir.dt.uint32
AF = mybir.ActivationFunctionType
OP = mybir.AluOpType
AX = mybir.AxisListType

ENGS = ("sync", "vector", "scalar", "tensor", "gpsimd")


class Sem:
    __slots__ = ("h", "name")

    def __init__(self, h, name):
        self.h = h
        self.name = name


class Prog:
    def __init__(self, nc):
        self.nc = nc
        self.streams = {e: [] for e in ENGS}
        self.counts = {}
        self.eng_sem = {}

    def bind(self, eng, sem):
        self.eng_sem[eng] = sem

    def sem(self, ctx, name):
        h = ctx.enter_context(self.nc.semaphore(name))
        self.counts[name] = 0
        return Sem(h, name)

    def emit(self, eng, builder, sem=None, inc=1, waits=()):
        waits = tuple(waits)
        own = self.eng_sem.get(eng)
        if own is not None:
            c = self.counts[own.name]
            if c > 0:
                waits = waits + ((own, c),)
        if sem is not None:
            self.counts[sem.name] += inc

        best = {}
        for s, c in waits:
            if c > 0 and (s.name not in best or best[s.name][1] < c):
                best[s.name] = (s, c)
        waits = tuple(best.values())

        def f(h):
            for s, c in waits:
                h.wait_ge(s.h, c)
            ins = builder(h)
            if sem is not None:
                ins.then_inc(sem.h, inc)
        self.streams[eng].append(f)

    def dma(self, eng, out, in_, sem, waits=()):
        self.emit(eng, lambda h, o=out, i=in_: h.dma_start(o, i),
                  sem=sem, inc=16, waits=waits)

    def wait(self, eng, *pairs):
        pairs = tuple(pairs)

        def f(h):
            for s, c in pairs:
                h.wait_ge(s.h, c)
        self.streams[eng].append(f)

    def n(self, sem):
        return self.counts[sem.name]

    def run(self, block):
        for eng in ENGS:
            fns = self.streams[eng]
            if not fns:
                continue

            def body(h, fns=fns):
                for f in fns:
                    f(h)
            getattr(block, eng)(body)



def emit_plain_ln(prog, sems, bufs, x_in, out, D, extra_in_waits=()):
    vsem, ssem = sems["vsem"], sems["ssem"]
    prog.emit("scalar", lambda h: h.activation(
        bufs["sq"], x_in, AF.Square),
        sem=ssem, waits=extra_in_waits)
    ws = prog.n(ssem)
    prog.emit("vector", lambda h: h.tensor_reduce(
        bufs["ssum"], x_in.rearrange("p (a b) -> p a b", b=32),
        axis=AX.XY, op=OP.add), sem=vsem, waits=extra_in_waits)
    prog.emit("vector", lambda h: h.tensor_reduce(
        bufs["ssq"], bufs["sq"].rearrange("p (a b) -> p a b", b=32),
        axis=AX.XY, op=OP.add), sem=vsem, waits=[(ssem, ws)])
    prog.emit("vector", lambda h: h.tensor_scalar(
        bufs["sm"], bufs["ssum"], -1.0 / D, None, op0=OP.mult), sem=vsem)
    prog.emit("vector", lambda h: h.tensor_tensor(
        bufs["sv"], bufs["sm"], bufs["sm"], op=OP.mult), sem=vsem)
    prog.emit("vector", lambda h: h.scalar_tensor_tensor(
        bufs["sv"], bufs["ssq"], 1.0 / D, bufs["sv"],
        op0=OP.mult, op1=OP.subtract), sem=vsem, waits=[(ssem, ws)])
    prog.emit("scalar", lambda h: h.activation(
        bufs["sr"], bufs["sv"], AF.Sqrt, bias=bufs["eps"]),
        sem=ssem, waits=[(vsem, prog.n(vsem))])
    prog.emit("vector", lambda h: h.reciprocal(bufs["sr"], bufs["sr"]),
              sem=vsem, waits=[(ssem, prog.n(ssem))])
    prog.emit("vector", lambda h: h.tensor_scalar(
        out, x_in, bufs["sm"], bufs["sr"], op0=OP.add, op1=OP.mult), sem=vsem)
    return prog.n(vsem), prog.n(ssem)


def emit_quant(prog, sems, bufs, x_in, out_bf, waits0=()):
    vsem, ssem = sems["vsem"], sems["ssem"]
    OPl = OP
    g = lambda ap: ap.rearrange("p (g e) -> p g e", e=32)
    prog.emit("vector", lambda h: h.tensor_reduce(
        bufs["wmax"], g(x_in), axis=AX.X, op=OPl.max,
        apply_absolute_value=True), sem=vsem, waits=tuple(waits0))
    prog.emit("vector", lambda h: h.tensor_scalar(
        bufs["scl"], bufs["wmax"], 1.0 / 7.5, 1e-30,
        op0=OPl.mult, op1=OPl.max), sem=vsem)
    prog.emit("vector", lambda h: h.reciprocal(bufs["rcp"], bufs["scl"]),
              sem=vsem)
    prog.emit("vector", lambda h: h.tensor_tensor(
        g(bufs["zt"]), g(x_in),
        bufs["rcp"].unsqueeze(2).broadcast_to([128, 32, 32]),
        op=OPl.mult), sem=vsem)
    prog.emit("scalar", lambda h: h.activation(
        bufs["i32t"], bufs["zt"], AF.Copy, scale=8.0),
        sem=ssem, waits=[(vsem, prog.n(vsem))])
    prog.emit("scalar", lambda h: h.activation(
        bufs["f8t"], bufs["zt"], AF.Copy), sem=ssem)
    prog.emit("scalar", lambda h: h.activation(
        bufs["az"], bufs["zt"], AF.Abs), sem=ssem)
    ws = prog.n(ssem)
    prog.emit("vector", lambda h: h.tensor_copy(bufs["i_f"], bufs["i32t"]),
              sem=vsem, waits=[(ssem, ws)])
    prog.emit("vector", lambda h: h.tensor_scalar(
        bufs["qs"], bufs["i_f"], 0.125, None, op0=OPl.mult), sem=vsem)
    prog.emit("vector", lambda h: h.tensor_copy(bufs["qb"], bufs["f8t"]),
              sem=vsem)
    prog.emit("vector", lambda h: h.tensor_scalar(
        bufs["msk"], bufs["az"], 2.0, None, op0=OPl.is_lt), sem=vsem)
    prog.emit("vector", lambda h: h.select(
        bufs["q"], bufs["msk"], bufs["qs"], bufs["qb"], add_drain=True),
        sem=vsem)
    prog.emit("vector", lambda h: h.tensor_tensor(
        g(out_bf), g(bufs["q"]),
        bufs["scl"].unsqueeze(2).broadcast_to([128, 32, 32]),
        op=OPl.mult), sem=vsem)
    return prog.n(vsem)


def prep_launch1_inputs(inputs, core):
    b, half = core // 2, core % 2
    x = np.asarray(inputs["L0_attn_x"][b])
    xp = np.asarray(inputs["L0_attn_x_proj_w"])
    dtw = np.asarray(inputs["L0_attn_dt_proj_w"])
    dtb = np.asarray(inputs["L0_attn_dt_proj_b"])
    alog = np.asarray(inputs["L0_attn_A_log"])
    dpar = np.asarray(inputs["L0_attn_D_param"])
    lo, hi = (0, 512) if half == 0 else (512, 1024)
    perm = np.r_[0:1024] if half == 0 else np.r_[512:1024, 0:512]
    xperm = np.ascontiguousarray(x[:, perm])
    xpT = np.ascontiguousarray(xp[:, perm].T)
    sl = slice(lo, hi)
    ident = np.eye(128)
    return dict(
        x=xperm.astype(np.float32),
        xprojT=np.ascontiguousarray(
            xpT.reshape(8, 128, 96).transpose(1, 0, 2).reshape(128, 768)
        ).astype(np.float32),
        dtwT=np.ascontiguousarray(dtw[sl].T).astype(np.float32),
        dtb=np.ascontiguousarray(dtb[sl].reshape(4, 128).T).astype(np.float32),
        alog=np.ascontiguousarray(
            alog[sl].reshape(4, 128, 16).transpose(1, 0, 2).reshape(128, 64)
        ).astype(np.float32),
        dpar=np.ascontiguousarray(dpar[sl].reshape(4, 128).T).astype(np.float32),
        identf=ident.astype(np.float32),
        identb=ident.astype(ml_dtypes.bfloat16),
    )


def prep_launch3_inputs(inputs, core, x2_full):
    b, th = core // 2, core % 2
    own = slice(2048 * th, 2048 * (th + 1))
    oth = slice(2048 * (1 - th), 2048 * (2 - th))
    x2 = np.asarray(x2_full[b])
    x2c = np.concatenate([x2[own], x2[oth]], 0)
    Wq = np.asarray(inputs["L1_attn_Wq"]); Wk = np.asarray(inputs["L1_attn_Wk"])
    Wv = np.asarray(inputs["L1_attn_Wv"]); Wo = np.asarray(inputs["L1_attn_Wo"])
    wqT = Wq.T; wkT = Wk.T; wvT = Wv.T; woT = Wo.T
    wvaT = np.zeros((1024, 1040), dtype=np.float64)
    for hh in range(16):
        wvaT[:, 65 * hh:65 * hh + 64] = wvT[:, 64 * hh:64 * (hh + 1)]
    bva = np.zeros(1040); bva[64::65] = 1.0
    bv = np.asarray(inputs["L1_attn_bv"])
    for hh in range(16):
        bva[65 * hh:65 * hh + 64] = bv[64 * hh:64 * (hh + 1)]
    ident = np.eye(128)
    f32 = np.float32
    return dict(
        x2full=np.ascontiguousarray(x2c).astype(f32),
        wk_l=np.ascontiguousarray(
            wkT.reshape(8, 128, 8, 128).transpose(1, 2, 0, 3).reshape(128, 8192)
        ).astype(f32),
        wv_l=np.ascontiguousarray(
            wvaT.reshape(8, 128, 1040).transpose(1, 0, 2).reshape(128, 8320)
        ).astype(f32),
        wq_l=np.ascontiguousarray(
            wqT.reshape(8, 128, 8, 128).transpose(2, 1, 0, 3).reshape(8, 128, 1024)
        ).astype(f32),
        wo_l=np.ascontiguousarray(
            woT.reshape(8, 128, 2, 512).transpose(2, 1, 0, 3).reshape(2, 128, 4096)
        ).astype(f32),
        bk=np.ascontiguousarray(
            np.asarray(inputs["L1_attn_bk"]).reshape(8, 128).T).astype(f32),
        bq=np.ascontiguousarray(
            np.asarray(inputs["L1_attn_bq"]).reshape(8, 128).T).astype(f32),
        bvaug=bva.reshape(1, 1040).astype(f32),
        bo=np.asarray(inputs["L1_attn_bo"]).reshape(1, 1024).astype(f32),
        identf=ident.astype(f32),
        identb=ident.astype(ml_dtypes.bfloat16),
    )




T, D, R, N = 4096, 1024, 64, 16
DH, NDC = 512, 4


class Arena:
    def __init__(self, handle, cols):
        self.h = handle
        self.cols = cols
        self.off = 0

    def al(self, cols, dtype=F32):
        w = {2: 1, 4: 2}[mybir.dt.size(dtype)]
        need = cols * w
        off = (self.off + 31) & ~31
        assert off + need <= self.cols, f"arena overflow {off}+{need}>{self.cols}"
        self.off = off + need
        view = self.h[:, off:off + need]
        if dtype != BF16:
            view = view.bitcast(dtype)
        return view

    def reset(self):
        self.off = 0


def barrier(prog, engs=("sync", "vector", "scalar", "tensor", "gpsimd"),
            skip=()):
    pairs = tuple((s, c) for s, c in
                  ((prog._sems[n], prog.counts[n]) for n in prog._sems
                   if n not in skip)
                  if c > 0)
    for e in engs:
        prog.wait(e, *pairs)


def build_fused():
    nc = bass.Bass("TRN2", target_bir_lowering=False)
    x_in = nc.dram_tensor("x", [T, D], F32, kind="ExternalInput").ap()
    xprojT = nc.dram_tensor("xprojT", [128, 8 * 96], F32, kind="ExternalInput").ap()
    dtwT = nc.dram_tensor("dtwT", [R, DH], F32, kind="ExternalInput").ap()
    dtb = nc.dram_tensor("dtb", [128, NDC], F32, kind="ExternalInput").ap()
    alog = nc.dram_tensor("alog", [128, NDC * N], F32, kind="ExternalInput").ap()
    dpar = nc.dram_tensor("dpar", [128, NDC], F32, kind="ExternalInput").ap()
    identf_d = nc.dram_tensor("identf", [128, 128], F32, kind="ExternalInput").ap()
    identb_d = nc.dram_tensor("identb", [128, 128], BF16, kind="ExternalInput").ap()
    x_res = nc.dram_tensor("x_res", [2048, D], F32, kind="ExternalInput").ap()
    lnab = nc.dram_tensor("lnab", [2, D], F32, kind="ExternalInput").ap()
    theta = nc.dram_tensor("theta", [D, D], F32, kind="ExternalInput").ap()
    thn = nc.dram_tensor("thn", [D, D], F32, kind="ExternalInput").ap()
    phi = nc.dram_tensor("phi", [D, D], F32, kind="ExternalInput").ap()
    phn = nc.dram_tensor("phn", [D, D], F32, kind="ExternalInput").ap()
    idx_y = nc.dram_tensor("idx_y", [128, 32], I32, kind="ExternalInput").ap()
    wk_l = nc.dram_tensor("wk_l", [128, 8192], F32, kind="ExternalInput").ap()
    wv_l = nc.dram_tensor("wv_l", [128, 8320], F32, kind="ExternalInput").ap()
    wq_l = nc.dram_tensor("wq_l", [8, 128, 1024], F32, kind="ExternalInput").ap()
    wo_l = nc.dram_tensor("wo_l", [2, 128, 4096], F32, kind="ExternalInput").ap()
    bk_in = nc.dram_tensor("bk", [128, 8], F32, kind="ExternalInput").ap()
    bq_in = nc.dram_tensor("bq", [128, 8], F32, kind="ExternalInput").ap()
    bv_in = nc.dram_tensor("bvaug", [1, 1040], F32, kind="ExternalInput").ap()
    bo_in = nc.dram_tensor("bo", [1, 1024], F32, kind="ExternalInput").ap()
    lt_in = nc.dram_tensor("lt", [128, 16 * 3 * 128], F32, kind="ExternalInput").ap()
    idx_vb = nc.dram_tensor("idx_vb", [128, 4 * 6], I32, kind="ExternalInput").ap()
    w4_in = nc.dram_tensor("weight", [D, D], F32, kind="ExternalInput").ap()
    b4_in = nc.dram_tensor("bias", [1, D], F32, kind="ExternalInput").ap()
    out_d = nc.dram_tensor("out_rows", [2048, D], F32, kind="ExternalOutput").ap()
    bc_dram = nc.dram_tensor("bc_dram", [32, T], BF16).ap()
    y_loc = nc.dram_tensor("y_loc", [T, DH], F32, kind="Internal").ap()
    y_cat = nc.dram_tensor("y_cat", [2 * T, DH], F32, kind="Internal").ap()
    x2_loc = nc.dram_tensor("x2_loc", [2048, D], F32, kind="Internal").ap()
    x2cat = nc.dram_tensor("x2cat", [T, D], F32, kind="Internal").ap()
    k_dram = nc.dram_tensor("k_dram", [8, 128, T], BF16).ap()
    xfm_dram = nc.dram_tensor("xfm_dram", [8, 128, 2048], BF16).ap()
    v_dram = nc.dram_tensor("v_dram", [T, 1040], BF16).ap()
    rsraw_dram = nc.dram_tensor("rsraw_dram", [1, 16 * 512], BF16).ap()
    rsf_dram = nc.dram_tensor("rsf_dram", [1, 16 * 512], F32).ap()
    nb_dram = nc.dram_tensor("nb_dram", [1, 16], F32).ap()
    x3_loc = nc.dram_tensor("x3_loc", [2048, D], F32, kind="Internal").ap()

    with ExitStack() as ctx:
        def sbt(name, shape, dtype=F32):
            return ctx.enter_context(nc.sbuf_tensor(name, shape, dtype))

        s_idf = sbt("s_idf", [128, 128], F32)
        s_idb = sbt("s_idb", [128, 128], BF16)
        s_x = [sbt(f"s_x{i}", [128, D]) for i in range(2)]
        s_z = sbt("s_z", [128, D])
        s_t = sbt("s_t", [128, D])
        st = {k: sbt("st_" + k, [128, 1], F32) for k in
              ("ssum", "ssq", "sm", "sv", "sr", "eps", "e12", "pih")}
        lnb = dict(sq=s_t[:, 0:D], ssum=st["ssum"][:], ssq=st["ssq"][:],
                   sm=st["sm"][:], sv=st["sv"][:], sr=st["sr"][:],
                   eps=st["eps"][:])
        ACOLS = 92928
        ar = Arena(sbt("arena", [128, ACOLS], BF16), ACOLS)
        pp = [ctx.enter_context(nc.psum_tensor(f"pp{i}", [128, 512], F32))
              for i in range(8)]

        prog = Prog(nc)
        prog._sems = {}

        def sem(name):
            s = prog.sem(ctx, name)
            prog._sems[name] = s
            return s

        dsem = sem("p1d"); vsem = sem("p1v"); ssem = sem("p1s"); tsem = sem("p1t")
        bsem2 = [sem("p1b0"), sem("p1b1")]; bcsem = sem("p1bc")
        gsem = sem("p1g")
        xs = [sem("p1x0"), sem("p1x1")]
        osm = [sem("p1o0"), sem("p1o1")]
        sems = dict(vsem=vsem, ssem=ssem)
        prog.bind("vector", vsem); prog.bind("scalar", ssem)

        x_bf = [ar.al(T, BF16) for _ in range(8)]
        s_xpj_all = ar.al(8 * 96, BF16)
        _dbc_reg = ar.al(T, BF16)
        deltar_bf = _dbc_reg
        bcpad_bf = _dbc_reg
        s_dtw_h = sbt("p1_dtw", [64, DH], F32)
        s_dtw = s_dtw_h[:]
        s_dtw_bf_h = sbt("p1_dtwb", [64, DH], BF16)
        s_dtw_bf = s_dtw_bf_h[:]
        s_dtb = ar.al(NDC, F32)
        s_alog = ar.al(NDC * N, F32)
        s_aneg = ar.al(NDC * N, F32)
        s_dpar = ar.al(NDC, F32)
        delta_dc = ar.al(T, F32)
        s_xpjf = delta_dc[:, 0:768]
        sp_t1 = ar.al(T, F32)
        s_xln = sp_t1[:, 0:D]
        lnb1 = dict(lnb); lnb1["sq"] = sp_t1[:, D:2 * D]
        dx_dc = ar.al(T, BF16)
        dA0 = ar.al(T, F32)
        h_t = ar.al(T, BF16)
        bxt0 = ar.al(T, BF16)
        hc0 = ar.al(T, BF16)
        b_rep0 = ar.al(T, BF16)
        c_rep0 = ar.al(T, BF16)
        y_stage = ar.al(512, F32)
        ytm = [ar.al(512, F32) for _ in range(2)]

        prog.dma("sync", s_idf[:], identf_d, dsem)
        prog.dma("sync", s_idb[:], identb_d, dsem)
        prog.dma("sync", s_dtw, dtwT, dsem)
        prog.dma("sync", s_dtb, dtb, dsem)
        prog.dma("sync", s_dpar, dpar, dsem)
        prog.dma("sync", s_alog, alog, dsem)
        prog.dma("sync", s_xpjf, xprojT, dsem)
        wconst = prog.n(dsem)
        prog.emit("vector", lambda h: h.memset(st["eps"][:], 1e-5), sem=vsem)
        prog.emit("vector", lambda h: h.memset(st["e12"][:], 1e-12), sem=vsem)
        prog.emit("vector", lambda h: h.memset(st["pih"][:], np.pi / 2), sem=vsem)
        prog.emit("vector", lambda h: h.tensor_copy(s_dtw_bf, s_dtw),
                  sem=vsem, waits=[(dsem, wconst)])
        prog.emit("scalar", lambda h: h.activation(
            s_aneg, s_alog, AF.Exp), sem=ssem, waits=[(dsem, wconst)])
        prog.emit("vector", lambda h: h.tensor_scalar(
            s_aneg, s_aneg, -1.0, None, op0=OP.mult),
            sem=vsem, waits=[(ssem, prog.n(ssem))])
        prog.emit("vector", lambda h: h.tensor_copy(
            s_xpj_all, s_xpjf), sem=vsem, waits=[(dsem, wconst)])

        NT = 32
        evac1_done = {}
        ln_done_v = [0] * NT
        sq_done_s = [0] * NT
        tp_prev = 0
        for tt in range(NT):
            buf = s_x[tt % 2]
            xsem = xs[tt % 2]
            w = []
            if tt >= 2:
                w = [(vsem, ln_done_v[tt - 2]), (ssem, sq_done_s[tt - 2])]
            prog.dma("sync", buf[:], x_in[128 * tt:128 * (tt + 1), :], xsem, waits=w)
            wd = prog.n(xsem)
            exw = [(xsem, wd)]
            if tt >= 1:
                exw.append((tsem, tp_prev))
            lv, ls = emit_plain_ln(prog, sems, lnb1, buf[:], s_xln, D,
                                   extra_in_waits=exw)
            ln_done_v[tt] = lv
            sq_done_s[tt] = ls
            for dc8 in range(8):
                pb = pp[dc8 % 2]
                g1 = tt * 8 + dc8
                tw = [(vsem, lv), (dsem, wconst)]
                if g1 >= 2:
                    tw.append((vsem, evac1_done[g1 - 2]))
                prog.emit("tensor", lambda h, pb=pb, dc8=dc8: h.transpose(
                    pb[:, 0:128], sp_t1[:, 128 * dc8:128 * (dc8 + 1)], s_idf[:]),
                    sem=tsem, waits=tw)
                prog.emit("vector", lambda h, pb=pb, dc8=dc8, tt=tt: h.tensor_copy(
                    x_bf[dc8][:, 128 * tt:128 * (tt + 1)], pb[:, 0:128]),
                    sem=vsem, waits=[(tsem, prog.n(tsem))])
                evac1_done[g1] = prog.n(vsem)
            tp_prev = prog.n(tsem)
        wxbf = prog.n(vsem)

        for nchunk in range(8):
            wv = prog.n(vsem)
            for kc in range(8):
                prog.emit("tensor", lambda h, nchunk=nchunk, kc=kc: h.matmul(
                    pp[2][0:96, :], s_xpj_all[:, 96 * kc:96 * (kc + 1)],
                    x_bf[kc][:, 512 * nchunk:512 * (nchunk + 1)],
                    start=(kc == 0), stop=(kc == 7)),
                    sem=tsem, waits=[(vsem, max(wxbf, wv))])
            wt = prog.n(tsem)
            prog.emit("vector", lambda h, nchunk=nchunk: h.tensor_copy(
                deltar_bf[0:64, 512 * nchunk:512 * (nchunk + 1)], pp[2][0:64, :]),
                sem=vsem, waits=[(tsem, wt)])
            prog.emit("vector", lambda h, nchunk=nchunk: h.tensor_copy(
                bcpad_bf[64:96, 512 * nchunk:512 * (nchunk + 1)], pp[2][64:96, :]),
                sem=vsem)
        wbc = prog.n(vsem)
        prog.dma("sync", bc_dram, bcpad_bf[64:96, :], bcsem, waits=[(vsem, wbc)])
        wbcd = prog.n(bcsem)

        _p1_state = {}
        hc_done = {}
        bx_done = {}
        scan_done = {}
        ymm_done = {}
        scan_prev = [0]
        devac_done = {}
        for dc in range(NDC):
            w_pre = prog.n(vsem)
            ws_pre = prog.n(ssem)
            for nchunk in range(8):
                bank = pp[2 + (nchunk % 2)]
                tw = [(vsem, max(w_pre, devac_done.get((dc, nchunk - 2), 0))),
                      (ssem, ws_pre)]
                prog.emit("tensor", lambda h, dc=dc, nchunk=nchunk, bank=bank:
                          h.matmul(
                              bank[:], s_dtw_bf[:, 128 * dc:128 * (dc + 1)],
                              deltar_bf[0:64, 512 * nchunk:512 * (nchunk + 1)],
                              start=True, stop=True),
                          sem=tsem, waits=tw)
                prog.emit("vector", lambda h, nchunk=nchunk, dc=dc, bank=bank:
                          h.tensor_scalar(
                              delta_dc[:, 512 * nchunk:512 * (nchunk + 1)],
                              bank[:], s_dtb[:, dc:dc + 1], None, op0=OP.add),
                          sem=vsem, waits=[(tsem, prog.n(tsem))])
                devac_done[(dc, nchunk)] = prog.n(vsem)
            wdl = prog.n(vsem)
            prog.emit("scalar", lambda h: h.activation(
                sp_t1, delta_dc, AF.Abs), sem=ssem, waits=[(vsem, wdl)])
            prog.emit("scalar", lambda h: h.activation(
                sp_t1, sp_t1, AF.Exp, scale=-1.0), sem=ssem)
            prog.emit("scalar", lambda h: h.activation(
                sp_t1, sp_t1, AF.Ln, bias=1.0), sem=ssem)
            wsp = prog.n(ssem)
            prog.emit("vector", lambda h: h.tensor_scalar(
                delta_dc, delta_dc, 0.0, None, op0=OP.max),
                sem=vsem, waits=[(ssem, wsp)])
            prog.emit("vector", lambda h: h.tensor_tensor(
                delta_dc, delta_dc, sp_t1, op=OP.add), sem=vsem)
            prog.emit("vector", lambda h, dc=dc: h.tensor_tensor(
                dx_dc, delta_dc, x_bf[dc][:, 0:T], op=OP.mult), sem=vsem)
            wdx = prog.n(vsem)

            dA_b = [dA0, sp_t1]
            bxt_b = [bxt0, x_bf[4]]
            hc_b = [hc0, x_bf[5]]
            brep_b = [b_rep0, x_bf[6]]
            crep_b = [c_rep0, x_bf[7]]
            if dc not in _p1_state:
                _p1_state[dc] = True
            for n in range(N):
                par = n % 2
                bsem = bsem2[par]
                b_rep, c_rep = brep_b[par], crep_b[par]
                dA, bxt, hc_t = dA_b[par], bxt_b[par], hc_b[par]
                prog.dma("sync", b_rep,
                         bc_dram[n:n + 1, :].broadcast_to([128, T]), bsem,
                         waits=[(bcsem, wbcd),
                                (vsem, hc_done.get((dc, n - 2), scan_prev[0]))])
                prog.dma("scalar", c_rep,
                         bc_dram[16 + n:17 + n, :].broadcast_to([128, T]), bsem,
                         waits=[(bcsem, wbcd),
                                (vsem, hc_done.get((dc, n - 2), scan_prev[0]))])
                wb = prog.n(bsem)
                prog.emit("scalar", lambda h, dc=dc, n=n, dA=dA: h.activation(
                    dA, delta_dc, AF.Exp,
                    scale=s_aneg[:, dc * N + n:dc * N + n + 1]),
                    sem=ssem,
                    waits=[(vsem, max(wdx,
                                      scan_done.get((dc, n - 2), scan_prev[0])))])
                wda = prog.n(ssem)
                prog.emit("vector", lambda h, bxt=bxt, b_rep=b_rep:
                          h.tensor_tensor(bxt, dx_dc, b_rep, op=OP.mult),
                          sem=vsem, waits=[(bsem, wb), (vsem, wdx)])
                bx_done[(dc, n)] = prog.n(vsem)
                prog.emit("vector", lambda h, dA=dA, bxt=bxt:
                          h.tensor_tensor_scan(
                              h_t, dA, bxt, 0.0, op0=OP.mult, op1=OP.add),
                          sem=vsem, waits=[(ssem, wda)])
                scan_done[(dc, n)] = prog.n(vsem)
                prog.emit("vector", lambda h, hc_t=hc_t, c_rep=c_rep:
                          h.tensor_tensor(hc_t, h_t, c_rep, op=OP.mult),
                          sem=vsem,
                          waits=[(tsem, ymm_done.get((dc, n - 2), 0))])
                whc = prog.n(vsem)
                hc_done[(dc, n)] = whc
                for tc in range(8):
                    prog.emit("tensor", lambda h, tc=tc, n=n, hc_t=hc_t:
                              h.matmul(
                                  pp[tc][:], s_idb[:],
                                  hc_t[:, 512 * tc:512 * (tc + 1)],
                                  start=(n == 0), stop=(n == N - 1)),
                              sem=tsem, waits=[(vsem, whc)])
                ymm_done[(dc, n)] = prog.n(tsem)
            scan_prev[0] = prog.n(vsem)
            wyt = prog.n(tsem)
            for tc in range(8):
                prog.emit("vector", lambda h, dc=dc, tc=tc: h.scalar_tensor_tensor(
                    y_stage,
                    x_bf[dc][:, 512 * tc:512 * (tc + 1)],
                    s_dpar[:, dc:dc + 1], pp[tc][:],
                    op0=OP.mult, op1=OP.add),
                    sem=vsem, waits=[(tsem, wyt)])
                wys = prog.n(vsem)
                par = tc % 2
                ob = ytm[par]
                for sub in range(4):
                    tw = [(vsem, wys)] if sub == 0 else [(vsem, prog.n(vsem))]
                    prog.emit("tensor", lambda h, tc=tc, sub=sub: h.transpose(
                        pp[tc][:, 0:128],
                        y_stage[:, 128 * sub:128 * (sub + 1)], s_idf[:]),
                        sem=tsem, waits=tw)
                    wtp = prog.n(tsem)
                    ow = [(tsem, wtp)]
                    if sub == 0:
                        ow.append((osm[par], prog.n(osm[par])))
                    prog.emit("vector", lambda h, sub=sub, ob=ob, tc=tc:
                              h.tensor_copy(ob[:, 128 * sub:128 * (sub + 1)],
                                            pp[tc][:, 0:128]),
                              sem=vsem, waits=ow)
                prog.dma("sync",
                         y_loc[512 * tc:512 * (tc + 1), 128 * dc:128 * (dc + 1)]
                         .rearrange("(s p) d -> p s d", s=4),
                         ob.rearrange("p (s d) -> p s d", s=4),
                         osm[par], waits=[(vsem, prog.n(vsem))])

        c1 = sem("c1")
        prog.emit("gpsimd", lambda h: h.collective_compute(
            "AllGather", OP.bypass,
            replica_groups=[[0, 1], [2, 3], [4, 5], [6, 7]],
            ins=[y_loc.opt()], outs=[y_cat.opt()]),
            sem=c1, inc=1,
            waits=[(osm[0], prog.n(osm[0])), (osm[1], prog.n(osm[1]))])
        barrier(prog, skip=("c1",))
        ar.reset()

        dsem = sem("p2d"); vsem = sem("p2v"); ssem = sem("p2s"); tsem = sem("p2t")
        ys2 = [sem("p2y0"), sem("p2y1")]
        xs2p = [sem("p2x0"), sem("p2x1")]
        wsem = sem("p2w"); osem = sem("p2o")
        ixsem = sem("p2ix")
        sems = dict(vsem=vsem, ssem=ssem)
        prog.bind("vector", vsem); prog.bind("scalar", ssem)

        s_y2 = [ar.al(D, F32) for _ in range(2)]
        lnbc = ar.al(2 * D, F32)
        x1_sb = ar.al(16 * D, F32)
        x1fm = [ar.al(2048, BF16) for _ in range(8)]
        wstg2 = ar.al(D, F32)
        wstg2b = ar.al(D, F32)
        cosb = [ar.al(D, BF16) for _ in range(8)]
        sinb = [ar.al(D, BF16) for _ in range(8)]
        r_sq = ar.al(512, F32)
        i_sq = ar.al(512, F32)
        r_sq2 = ar.al(512, F32)
        i_sq2 = ar.al(512, F32)
        ffn_t = ar.al(D, F32)
        s_idx = ar.al(32, I32)

        prog.dma("sync", lnbc[:, 0:D], lnab[0:1, :].broadcast_to([128, D]), dsem)
        prog.dma("sync", lnbc[:, D:2 * D], lnab[1:2, :].broadcast_to([128, D]), dsem)
        prog.dma("sync", s_idx, idx_y, ixsem)
        wc2 = prog.n(dsem)
        wix = prog.n(ixsem)

        for c in range(8):
            prog.dma("sync", wstg2, theta[128 * c:128 * (c + 1), :], wsem,
                     waits=[(vsem, prog.n(vsem)), (ssem, prog.n(ssem))])
            prog.dma("sync", wstg2b, thn[128 * c:128 * (c + 1), :], wsem)
            wwd = prog.n(wsem)
            prog.emit("vector", lambda h: h.tensor_tensor(
                wstg2, wstg2, wstg2b, op=OP.add),
                sem=vsem, waits=[(wsem, wwd)])
            prog.emit("scalar", lambda h, c=c: h.activation(
                cosb[c], wstg2, AF.Sin, bias=st["pih"][:]),
                sem=ssem, waits=[(vsem, prog.n(vsem))])
            prog.dma("sync", wstg2, phi[128 * c:128 * (c + 1), :], wsem,
                     waits=[(vsem, prog.n(vsem)), (ssem, prog.n(ssem))])
            prog.dma("sync", wstg2b, phn[128 * c:128 * (c + 1), :], wsem)
            wwd = prog.n(wsem)
            prog.emit("vector", lambda h: h.tensor_tensor(
                wstg2, wstg2, wstg2b, op=OP.add),
                sem=vsem, waits=[(wsem, wwd)])
            prog.emit("scalar", lambda h, c=c: h.activation(
                sinb[c], wstg2, AF.Sin), sem=ssem,
                waits=[(vsem, prog.n(vsem))])
        wtrig = prog.n(ssem)

        NT2 = 16
        s_tB = ar.al(D, F32)
        s_zB = ar.al(D, F32)
        stB = {k: ar.al(1, F32) for k in ("ssum", "ssq", "sm", "sv", "sr")}
        lnbB = dict(sq=s_tB, ssum=stB["ssum"], ssq=stB["ssq"],
                    sm=stB["sm"], sv=stB["sv"], sr=stB["sr"],
                    eps=st["eps"][:])
        lnbA = dict(lnb)
        tp2 = {}
        yread_done = {}
        bdone = {}
        wdx = {}

        def y_stage_a(tt):
            p = tt % 2
            buf = s_y2[p]
            ysem = ys2[p]
            stp = st if p == 0 else None
            sq_t = s_t[:, 0:D] if p == 0 else s_tB
            ssum = st["ssum"][:] if p == 0 else stB["ssum"]
            ssq = st["ssq"][:] if p == 0 else stB["ssq"]
            sm = st["sm"][:] if p == 0 else stB["sm"]
            sv = st["sv"][:] if p == 0 else stB["sv"]
            sr = st["sr"][:] if p == 0 else stB["sr"]
            w = [(c1, 1), (ixsem, wix)]
            if tt >= 2:
                w.append((vsem, yread_done[tt - 2]))
            prog.emit("gpsimd", lambda h, buf=buf, tt=tt: h.indirect_dma_start(
                buf[:, 0:512], None, y_cat,
                IndirectOffsetOnAxis(ap=s_idx[:, 2 * tt:2 * tt + 1], axis=0)),
                sem=ysem, inc=16, waits=w)
            prog.emit("gpsimd", lambda h, buf=buf, tt=tt: h.indirect_dma_start(
                buf[:, 512:1024], None, y_cat,
                IndirectOffsetOnAxis(ap=s_idx[:, 2 * tt + 1:2 * tt + 2], axis=0)),
                sem=ysem, inc=16)
            prog.dma("sync", s_x[p][:], x_res[128 * tt:128 * (tt + 1), :],
                     xs2p[p], waits=[(vsem, prog.n(vsem))] if tt >= 1 else [])
            wd = (ysem, prog.n(ysem))
            wdx[tt] = (xs2p[p], prog.n(xs2p[p]))
            prog.emit("scalar", lambda h, buf=buf, sq_t=sq_t: h.activation(
                sq_t, buf[:, 0:D], AF.Square),
                sem=ssem, waits=[wd, (vsem, bdone.get(tt - 2, 0))])
            ws = prog.n(ssem)
            prog.emit("vector", lambda h, buf=buf, ssum=ssum: h.tensor_reduce(
                ssum, buf[:, 0:D].rearrange("p (a b) -> p a b", b=32),
                axis=AX.XY, op=OP.add), sem=vsem,
                waits=[wd, (vsem, bdone.get(tt - 2, 0))])
            prog.emit("vector", lambda h, sq_t=sq_t, ssq=ssq: h.tensor_reduce(
                ssq, sq_t.rearrange("p (a b) -> p a b", b=32),
                axis=AX.XY, op=OP.add), sem=vsem, waits=[(ssem, ws)])
            prog.emit("vector", lambda h, sm=sm, ssum=ssum: h.tensor_scalar(
                sm, ssum, -1.0 / D, None, op0=OP.mult),
                sem=vsem)
            prog.emit("vector", lambda h, sv=sv, sm=sm: h.tensor_tensor(
                sv, sm, sm, op=OP.mult), sem=vsem)
            prog.emit("vector", lambda h, sv=sv, ssq=ssq: h.scalar_tensor_tensor(
                sv, ssq, 1.0 / D, sv,
                op0=OP.mult, op1=OP.subtract), sem=vsem)
            prog.emit("scalar", lambda h, sr=sr, sv=sv: h.activation(
                sr, sv, AF.Sqrt, scale=D / (D - 1.0)),
                sem=ssem, waits=[(vsem, prog.n(vsem))])
            prog.emit("vector", lambda h, sr=sr: h.tensor_tensor(
                sr, sr, st["e12"][:], op=OP.add),
                sem=vsem, waits=[(ssem, prog.n(ssem))])
            prog.emit("vector", lambda h, sr=sr: h.reciprocal(sr, sr),
                      sem=vsem)

        def y_stage_b(tt):
            p = tt % 2
            buf = s_y2[p]
            sz = s_z[:] if p == 0 else s_zB
            sm = st["sm"][:] if p == 0 else stB["sm"]
            sr = st["sr"][:] if p == 0 else stB["sr"]
            lnbp = lnbA if p == 0 else lnbB
            prog.emit("vector", lambda h, buf=buf, sz=sz, sm=sm, sr=sr:
                      h.tensor_scalar(
                          sz, buf[:, 0:D], sm, sr,
                          op0=OP.add, op1=OP.mult), sem=vsem,
                      waits=[(tsem, tp2.get(tt - 2, 0))])
            yread_done[tt] = prog.n(vsem)
            prog.emit("vector", lambda h, sz=sz: h.tensor_tensor(
                sz, sz, lnbc[:, 0:D], op=OP.mult), sem=vsem,
                waits=[(dsem, wc2)])
            prog.emit("vector", lambda h, sz=sz: h.tensor_tensor(
                sz, sz, lnbc[:, D:2 * D], op=OP.add), sem=vsem)
            emit_plain_ln(prog, sems, lnbp, sz, sz, D,
                          extra_in_waits=[(vsem, prog.n(vsem))])
            prog.emit("vector", lambda h, tt=tt, sz=sz: h.tensor_tensor(
                x1_sb[:, D * tt:D * (tt + 1)], s_x[p][:], sz, op=OP.add),
                sem=vsem, waits=[wdx[tt]])
            emit_plain_ln(prog, sems, lnbp,
                          x1_sb[:, D * tt:D * (tt + 1)], sz, D,
                          extra_in_waits=[(vsem, prog.n(vsem))])
            wn = prog.n(vsem)
            for dc8 in range(8):
                pb = pp[dc8 % 2]
                tw = [(vsem, wn)]
                if tt * 8 + dc8 >= 1:
                    tw.append((vsem, prog.n(vsem)))
                prog.emit("tensor", lambda h, pb=pb, dc8=dc8, sz=sz: h.transpose(
                    pb[:, 0:128], sz[:, 128 * dc8:128 * (dc8 + 1)], s_idf[:]),
                    sem=tsem, waits=tw)
                prog.emit("vector", lambda h, pb=pb, dc8=dc8, tt=tt: h.tensor_copy(
                    x1fm[dc8][:, 128 * tt:128 * (tt + 1)], pb[:, 0:128]),
                    sem=vsem, waits=[(tsem, prog.n(tsem))])
            tp2[tt] = prog.n(tsem)
            bdone[tt] = prog.n(vsem)

        y_stage_a(0)
        for tt in range(NT2):
            if tt + 1 < NT2:
                y_stage_a(tt + 1)
            y_stage_b(tt)
        wxfm = prog.n(vsem)


        fsq = {}
        for tt in range(NT2):
            for oc in range(2):
                idx = 2 * tt + oc
                bre, bim = pp[2 + 2 * (oc % 2)], pp[3 + 2 * (oc % 2)]
                rs_, is_ = (r_sq, i_sq) if idx % 2 == 0 else (r_sq2, i_sq2)
                for kc in range(8):
                    prog.emit("tensor", lambda h, kc=kc, tt=tt, oc=oc, bre=bre:
                              h.matmul(bre[:], x1fm[kc][:, 128 * tt:128 * (tt + 1)],
                                       cosb[kc][:, 512 * oc:512 * (oc + 1)],
                                       start=(kc == 0), stop=(kc == 7)),
                              sem=tsem,
                              waits=[(vsem, wxfm), (ssem, wtrig),
                                     (ssem, fsq.get(idx - 2, 0))])
                    prog.emit("tensor", lambda h, kc=kc, tt=tt, oc=oc, bim=bim:
                              h.matmul(bim[:], x1fm[kc][:, 128 * tt:128 * (tt + 1)],
                                       sinb[kc][:, 512 * oc:512 * (oc + 1)],
                                       start=(kc == 0), stop=(kc == 7)),
                              sem=tsem)
                wt = prog.n(tsem)
                prog.emit("scalar", lambda h, bre=bre, rs_=rs_: h.activation(
                    rs_, bre[:], AF.Square), sem=ssem,
                    waits=[(tsem, wt), (vsem, prog.n(vsem))])
                prog.emit("scalar", lambda h, bim=bim, is_=is_: h.activation(
                    is_, bim[:], AF.Square), sem=ssem)
                fsq[idx] = prog.n(ssem)
                prog.emit("vector", lambda h, rs_=rs_, is_=is_: h.tensor_tensor(
                    rs_, rs_, is_, op=OP.add),
                    sem=vsem, waits=[(ssem, prog.n(ssem))])
                prog.emit("scalar", lambda h, oc=oc, rs_=rs_: h.activation(
                    ffn_t[:, 512 * oc:512 * (oc + 1)], rs_, AF.Sqrt),
                    sem=ssem, waits=[(vsem, prog.n(vsem)), (osem, prog.n(osem))])
            wsq = prog.n(ssem)
            emit_plain_ln(prog, sems, lnb, ffn_t, ffn_t, D,
                          extra_in_waits=[(ssem, wsq)])
            prog.emit("vector", lambda h, tt=tt: h.tensor_tensor(
                ffn_t, ffn_t, x1_sb[:, D * tt:D * (tt + 1)], op=OP.add),
                sem=vsem, waits=[(osem, prog.n(osem))])
            prog.dma("sync", x2_loc[128 * tt:128 * (tt + 1), :], ffn_t, osem,
                     waits=[(vsem, prog.n(vsem))])

        c2 = sem("c2")
        prog.emit("gpsimd", lambda h: h.collective_compute(
            "AllGather", OP.bypass,
            replica_groups=[[0, 1], [2, 3], [4, 5], [6, 7]],
            ins=[x2_loc.opt()], outs=[x2cat.opt()]),
            sem=c2, inc=1, waits=[(osem, prog.n(osem))])
        barrier(prog, skip=("c2",))
        ar.reset()
        _phase3_4(prog, ar, pp, lnb, st, s_x, s_z, s_t, s_idf, s_idb, sem, sbt,
                  c2, x2cat, x2_loc, k_dram, xfm_dram, v_dram, rsraw_dram,
                  rsf_dram, nb_dram, x3_loc, out_d,
                  wk_l, wv_l, wq_l, wo_l, bk_in, bq_in, bv_in, bo_in, lt_in,
                  idx_vb, w4_in, b4_in)
        block = ctx.enter_context(nc.Block())
        prog.run(block)
    return nc


def _phase3_4(prog, ar, pp, lnb, st, s_x, s_z, s_t, s_idf, s_idb, sem,
              sbt,
              
              c2, x2cat, x2_loc, k_dram, xfm_dram, v_dram, rsraw_dram,
              rsf_dram, nb_dram, x3_loc, out_d,
              wk_l, wv_l, wq_l, wo_l, bk_in, bq_in, bv_in, bo_in, lt_in,
              idx_vb, w4_in, b4_in):
    D = 1024
    T_ = 4096
    dsem = sem("p3d"); vsem = sem("p3v"); ssem = sem("p3s"); tsem = sem("p3t")
    xs3 = [sem("p3x0"), sem("p3x1")]
    wsem = sem("p3w"); kw = sem("p3kw")
    kwk = [sem("p3kwa"), sem("p3kwb")]
    xqs = [sem("p3xqa"), sem("p3xqb")]
    kr = [sem("p3k0"), sem("p3k1")]
    xqw = sem("p3xq"); wqw = sem("p3wq"); wow = sem("p3wo")
    sh = sem("p3sh"); rsm = sem("p3rs"); vbw = sem("p3vb")
    osem3 = sem("p3o"); xrs = sem("p3xr"); ivx = sem("p3ivx")
    sems = dict(vsem=vsem, ssem=ssem)
    prog.bind("vector", vsem); prog.bind("scalar", ssem)

    xfm4 = ar.al(4096, BF16)
    wstg = ar.al(1792, F32)
    ovl = ar.al(16512, BF16)
    wkA = ovl[:, 0:8192]
    wvA = ovl[:, 8192:16512]
    vtm = ar.al(32 * 1040, BF16)
    kev = ar.al(512, BF16)
    kev2 = ar.al(512, BF16)
    s_bk = ar.al(8, F32)
    s_bq = ar.al(8, F32)
    bo_bc = ar.al(1024, BF16)
    slt_b = [ar.al(12 * 128, BF16), ar.al(12 * 128, BF16)]
    bv_bc = slt_b[0][:, 0:1040]
    ksb = [ovl[:, 0:4096], ovl[:, 4096:8192]]
    q_sb = ovl[:, 8192:12288]
    xq_sb = ovl[:, 12288:16384]
    wq_sb = ar.al(1024, BF16)
    wo_sb = ar.al(8192, BF16)
    expt = [ar.al(512, BF16) for _ in range(4)]
    att = xfm4
    g3 = ar.al(4096, BF16)
    v_band = ar.al(6 * 1040, BF16)
    rsreg = ar.al(8192, BF16)
    o_tm4 = rsreg.bitcast(F32)
    o_tm = s_z[0:128, 0:D]
    ff_t = sbt("p3_ffc", [128, 512], F32)
    ff_c = ff_t[:]
    rs16b_h = sbt("p3_rs16b", [16, 512], BF16)
    rs16_h = sbt("p3_rs16", [16, 512], F32)
    rs16b = rs16b_h[:]
    rs16 = rs16_h[:]
    small_h = sbt("p3_small", [128, 57], F32)
    kmax = small_h[0:2, 0:8]
    kred = small_h[0:2, 8:9]
    qmax = small_h[0:2, 9:17]
    nbias = small_h[0:2, 17:25]
    nb_bc = small_h[:, 25:41]
    nb_bc2 = small_h[:, 41:57]
    s_ivb_h = sbt("p3_ivb", [128, 24], I32)
    s_ivb = s_ivb_h[:]

    for dst, src_ in [(s_bk, bk_in), (s_bq, bq_in)]:
        prog.dma("sync", dst, src_, dsem)
    prog.dma("sync", wstg[:, 0:1040], bv_in.broadcast_to([128, 1040]), dsem)
    prog.dma("sync", s_ivb, idx_vb, dsem)
    wc1 = prog.n(dsem)
    prog.emit("vector", lambda h: h.tensor_copy(bv_bc, wstg[:, 0:1040]),
              sem=vsem, waits=[(dsem, wc1)])
    prog.dma("sync", wstg[:, 0:1024], bo_in.broadcast_to([128, 1024]), dsem,
             waits=[(vsem, prog.n(vsem))])
    prog.emit("vector", lambda h: h.tensor_copy(bo_bc, wstg[:, 0:1024]),
              sem=vsem, waits=[(dsem, prog.n(dsem))])
    for jc in range(2):
        for lo, hi in [(0, 1792), (1792, 3584), (3584, 4096)]:
            prog.dma("sync", wstg[:, 0:hi - lo], wo_l[jc][:, lo:hi], wsem,
                     waits=[(vsem, prog.n(vsem))])
            prog.emit("vector", lambda h, jc=jc, lo=lo, hi=hi: h.tensor_copy(
                wo_sb[:, 4096 * jc + lo:4096 * jc + hi], wstg[:, 0:hi - lo]),
                sem=vsem, waits=[(wsem, prog.n(wsem))])
    for part in range(5):
        lo, hi = 1792 * part, min(1792 * (part + 1), 8192)
        prog.dma("sync", wstg[:, 0:hi - lo], wk_l[:, lo:hi], wsem,
                 waits=[(vsem, prog.n(vsem))])
        prog.emit("vector", lambda h, lo=lo, hi=hi: h.tensor_copy(
            wkA[:, lo:hi], wstg[:, 0:hi - lo]),
            sem=vsem, waits=[(wsem, prog.n(wsem))])
    for part in range(5):
        lo, hi = 1792 * part, min(1792 * (part + 1), 8320)
        prog.dma("sync", wstg[:, 0:hi - lo], wv_l[:, lo:hi], wsem,
                 waits=[(vsem, prog.n(vsem))])
        prog.emit("vector", lambda h, lo=lo, hi=hi: h.tensor_copy(
            wvA[:, lo:hi], wstg[:, 0:hi - lo]),
            sem=vsem, waits=[(wsem, prog.n(wsem))])
    wk_ready = prog.n(vsem)
    wwo_res = prog.n(vsem)
    prog.emit("vector", lambda h: h.memset(kmax, 0.0), sem=vsem)

    xfm_b = [xfm4, g3]
    tpb = [pp[0], pp[1], pp[7]]
    kevac_done = {}
    kdma_done = {}
    ksq_done = {}
    knmm_done = {}
    knred_done = {}
    vevac_done = {}
    ln_v_done = {}
    ln_s_done = {}
    tcopy_done = {}
    tile_tp = {}
    xfm_ready = {}
    kv_read = {}
    xq_dma = {}
    tfi = [0]

    def emit_tile(tt, src, col_ti, xfm, whaz_dma, whaz_c):
        buf = s_x[tt % 2]
        xsem = xs3[tt % 2]
        w = list(whaz_dma)
        if tt >= 2:
            w += [(vsem, ln_v_done[tt - 2]), (ssem, ln_s_done[tt - 2])]
        prog.dma("sync", buf[:], src, xsem, waits=w)
        exw = [(xsem, prog.n(xsem)), (tsem, tile_tp.get(tt - 1, 0))]
        emit_plain_ln(prog, sems, lnb, buf[:], s_z[:], D, extra_in_waits=exw)
        ln_v_done[tt] = prog.n(vsem)
        ln_s_done[tt] = prog.n(ssem)
        wn = prog.n(vsem)
        for dc8 in range(8):
            fi = tfi[0]
            pb = tpb[fi % 3]
            tw = [(vsem, wn), (dsem, wc1)]
            if fi >= 3:
                tw.append((vsem, tcopy_done[fi - 3]))
            prog.emit("tensor", lambda h, pb=pb, dc8=dc8: h.transpose(
                pb[:, 0:128], s_z[:, 128 * dc8:128 * (dc8 + 1)], s_idf[:]),
                sem=tsem, waits=tw)
            cw = [(tsem, prog.n(tsem))]
            if dc8 == 0:
                cw += list(whaz_c)
            prog.emit("vector",
                      lambda h, pb=pb, dc8=dc8, col_ti=col_ti, xfm=xfm:
                      h.tensor_copy(
                          xfm[:, 512 * dc8 + 128 * col_ti:
                              512 * dc8 + 128 * (col_ti + 1)],
                          pb[:, 0:128]),
                      sem=vsem, waits=cw)
            tcopy_done[fi] = prog.n(vsem)
            tfi[0] += 1
        tile_tp[tt] = prog.n(tsem)

    def emit_k(g, dqc):
        xfm = xfm_b[g % 2]
        bank = pp[2 + (dqc % 2)]
        kv = kev if dqc % 2 == 0 else kev2
        wbase = max(xfm_ready[g], wk_ready)
        for c in range(8):
            tw = [(vsem, max(wbase, kevac_done.get(g * 8 + dqc - 2, 0)))]
            prog.emit("tensor", lambda h, dqc=dqc, c=c, bank=bank, xfm=xfm:
                      h.matmul(
                          bank[:],
                          wkA[:, 1024 * dqc + 128 * c:1024 * dqc + 128 * (c + 1)],
                          xfm[:, 512 * c:512 * (c + 1)],
                          start=(c == 0), stop=(c == 7)),
                      sem=tsem, waits=tw)
        wt = prog.n(tsem)
        prog.emit("vector", lambda h, dqc=dqc, bank=bank, kv=kv:
                  h.tensor_scalar(
                      kv, bank[:], s_bk[:, dqc:dqc + 1], None, op0=OP.add),
                  sem=vsem,
                  waits=[(tsem, wt),
                         kdma_done.get(g * 8 + dqc - 2, (kw, 0)),
                         (ssem, ksq_done.get(g * 8 + dqc - 2, 0))])
        wkev = prog.n(vsem)
        kevac_done[g * 8 + dqc] = wkev
        prog.dma("gpsimd", k_dram[dqc, :, 512 * g:512 * (g + 1)], kv,
                 kwk[dqc % 2], waits=[(vsem, wkev)])
        kdma_done[g * 8 + dqc] = (kwk[dqc % 2], prog.n(kwk[dqc % 2]))
        prog.emit("scalar", lambda h, kv=kv: h.activation(
            expt[0], kv, AF.Square), sem=ssem,
            waits=[(vsem, wkev), (tsem, knmm_done.get(g * 8 + dqc - 1, 0))])
        ksq_done[g * 8 + dqc] = prog.n(ssem)
        prog.emit("tensor", lambda h: h.matmul(
            pp[6][0:2, :], s_idb[:, 0:2], expt[0],
            start=True, stop=True), sem=tsem,
            waits=[(ssem, prog.n(ssem)),
                   (vsem, knred_done.get(g * 8 + dqc - 1, 0))])
        knmm_done[g * 8 + dqc] = prog.n(tsem)
        prog.emit("vector", lambda h: h.tensor_reduce(
            kred, pp[6][0:2, :], axis=AX.X, op=OP.max),
            sem=vsem, waits=[(tsem, prog.n(tsem))])
        prog.emit("vector", lambda h, dqc=dqc: h.tensor_tensor(
            kmax[:, dqc:dqc + 1], kmax[:, dqc:dqc + 1], kred,
            op=OP.max), sem=vsem)
        knred_done[g * 8 + dqc] = prog.n(vsem)

    def emit_v(g, ti, oc):
        xfm = xfm_b[g % 2]
        lo = 512 * oc
        hi = min(512 * (oc + 1), 1040)
        vidx = ti * 3 + oc
        bank = pp[4 + (vidx % 2)]
        wbase = max(xfm_ready[g], wk_ready)
        for c in range(8):
            tw = [(vsem, max(wbase, vevac_done.get(g * 12 + vidx - 2, 0)))]
            prog.emit("tensor", lambda h, ti=ti, c=c, lo=lo, hi=hi,
                      bank=bank, xfm=xfm: h.matmul(
                          bank[:, 0:hi - lo],
                          xfm[:, 512 * c + 128 * ti:512 * c + 128 * (ti + 1)],
                          wvA[:, 1040 * c + lo:1040 * c + hi],
                          start=(c == 0), stop=(c == 7)),
                      sem=tsem, waits=tw)
        prog.emit("vector", lambda h, g=g, ti=ti, lo=lo, hi=hi, bank=bank:
                  h.scalar_tensor_tensor(
                      vtm[:, 1040 * (4 * g + ti) + lo:1040 * (4 * g + ti) + hi],
                      bank[:, 0:hi - lo], 1.0,
                      bv_bc[:, lo:hi],
                      op0=OP.mult, op1=OP.add),
                  sem=vsem, waits=[(tsem, prog.n(tsem))])
        vevac_done[g * 12 + vidx] = prog.n(vsem)
        if oc == 2:
            wvt = prog.n(vsem)
            prog.dma("gpsimd",
                     v_dram[128 * (4 * g + ti):128 * (4 * g + ti + 1), :],
                     vtm[:, 1040 * (4 * g + ti):1040 * (4 * g + ti + 1)], kw,
                     waits=[(vsem, wvt)])

    for ti in range(4):
        emit_tile(ti, x2_loc[128 * ti:128 * (ti + 1), :], ti, xfm_b[0], [], [])
    xfm_ready_loc = {0: prog.n(vsem)}
    for lg in range(4):
        xfm = xfm_b[lg % 2]
        wgq = xfm_ready_loc[lg]
        for c in range(8):
            prog.dma("gpsimd", xfm_dram[c, :, 512 * lg:512 * (lg + 1)],
                     xfm[:, 512 * c:512 * (c + 1)], xqs[lg % 2],
                     waits=[(vsem, wgq)])
            if c in (1, 3, 5, 7):
                ti = c // 2
                if lg < 3:
                    ltt = 4 * (lg + 1) + ti
                    whaz_c = ([] if lg == 0 else
                              [(xqs[(lg - 1) % 2], xq_dma[lg - 1])])
                    emit_tile(ltt, x2_loc[128 * ltt:128 * (ltt + 1), :],
                              ti, xfm_b[(lg + 1) % 2], [], whaz_c)
                else:
                    emit_tile(16 + ti, x2cat[128 * ti:128 * (ti + 1), :],
                              ti, xfm_b[0],
                              [(c2, 1)] if ti < 2 else [],
                              [(xqs[0], xq_dma[2])])
        xq_dma[lg] = prog.n(xqs[lg % 2])
        if lg < 3:
            xfm_ready_loc[lg + 1] = prog.n(vsem)
    xfm_ready[0] = prog.n(vsem)

    for g in range(8):
        kv_units = [("k", dqc) for dqc in range(8)] + \
                   [("v", ti, oc) for ti in range(4) for oc in range(3)]
        if g < 7:
            nxt = [(16 + 4 * (g + 1) + ti,
                    x2cat[128 * (4 * (g + 1) + ti):
                          128 * (4 * (g + 1) + ti + 1), :],
                    ti, xfm_b[(g + 1) % 2],
                    [], [(tsem, kv_read[g - 1])] if g >= 1 else
                    [(xqs[1], xq_dma[3])])
                   for ti in range(4)]
        else:
            nxt = []
        ni = 0
        for i, u in enumerate(kv_units):
            if u[0] == "k":
                emit_k(g, u[1])
            else:
                emit_v(g, u[1], u[2])
            if i % 5 == 3 and ni < len(nxt):
                emit_tile(*nxt[ni])
                ni += 1
        while ni < len(nxt):
            emit_tile(*nxt[ni])
            ni += 1
        kv_read[g] = prog.n(tsem)
        xfm_ready[g + 1] = prog.n(vsem)
    wpassa_k = prog.n(kw)
    wpassa_ka = prog.n(kwk[0])
    wpassa_kb = prog.n(kwk[1])
    wpassa_xa = prog.n(xqs[0])
    wpassa_xb = prog.n(xqs[1])

    prog.emit("scalar", lambda h: h.activation(
        kmax, kmax, AF.Sqrt), sem=ssem, waits=[(vsem, prog.n(vsem))])
    wkmax = prog.n(ssem)

    nbsm = sem("p3nbs")
    rs2 = [sem("p3r0"), sem("p3r1")]
    lts = sem("p3lts")
    wqg = sem("p3wqg")
    nbb = [nb_bc, nb_bc2]
    avA, avB = pp[4], pp[5]
    qmm_done = {}
    qevac_all = {}
    wq_copy_last = {}
    lt_copy_last = {}
    wnb_q = {}
    exp_fin = {}
    av_fin = {}
    sc6 = {}
    scmm31 = {}
    shift_done = {}
    wvb_q = {}
    wrf_q = {}
    wlv_all = {}
    lv_mm_all = {}
    cmb_done = {}
    oproj_cd = {}
    oproj_all = {}
    oproj_ev = {}
    wev_q = {}
    rs_recip = {}
    ln_s_last = {}
    qst = {"redmm": 0}
    qpst = {}
    scd = {}
    qmax_done = {}

    def emit_qproj(q):
        xw = [(kw, wpassa_k), (kwk[0], wpassa_ka), (kwk[1], wpassa_kb),
              (xqs[0], wpassa_xa), (xqs[1], wpassa_xb),
              (tsem, prog.n(tsem))]
        kpw = [(kw, wpassa_k), (kwk[0], wpassa_ka), (kwk[1], wpassa_kb),
               (tsem, prog.n(tsem))]
        lcl = 0
        for lo, hi in [(0, 1024), (1024, 1536)]:
            prog.dma("scalar", wstg[:, 0:hi - lo],
                     lt_in[:, 1536 * q + lo:1536 * q + hi], lts,
                     waits=[(lts, prog.n(lts)), (vsem, lcl)])
            prog.emit("vector", lambda h, q=q, lo=lo, hi=hi: h.tensor_copy(
                slt_b[q % 2][:, lo:hi], wstg[:, 0:hi - lo]),
                sem=vsem, waits=[(lts, prog.n(lts))])
            lcl = prog.n(vsem)
        lt_copy_last[q] = lcl
        for c in range(8):
            prog.dma("scalar", xq_sb[:, 512 * c:512 * (c + 1)],
                     xfm_dram[c, :, 512 * q:512 * (q + 1)], xqw, waits=xw)
        wxq = prog.n(xqw)
        prog.dma("scalar", ksb[0], k_dram[0], kr[0], waits=kpw)
        qpst[q] = {"wqc": lcl, "wxq": wxq}
        for dqc in range(8):
            qp_unit(q, dqc)
        qp_post(q)

    def qp_pre(q):
        xw = [(tsem, qmm_done[q - 1])]
        for c in range(8):
            prog.dma("sync", xq_sb[:, 512 * c:512 * (c + 1)],
                     xfm_dram[c, :, 512 * q:512 * (q + 1)], xqw, waits=xw)
        qpst[q] = {"wqc": 0, "wxq": prog.n(xqw)}

    def qp_lt(q):
        lcl = wq_copy_last.get(q - 1, 0)
        for lo, hi in [(0, 1024), (1024, 1536)]:
            prog.dma("scalar", wstg[:, 0:hi - lo],
                     lt_in[:, 1536 * q + lo:1536 * q + hi], lts,
                     waits=[(lts, prog.n(lts)), (vsem, lcl),
                            (tsem, lv_mm_all.get(q - 2, 0))])
            prog.emit("vector", lambda h, q=q, lo=lo, hi=hi: h.tensor_copy(
                slt_b[q % 2][:, lo:hi], wstg[:, 0:hi - lo]),
                sem=vsem, waits=[(lts, prog.n(lts))])
            lcl = prog.n(vsem)
        lt_copy_last[q] = lcl
        qpst[q]["wqc"] = lcl

    def qp_ksb(q):
        prog.dma("scalar", ksb[0], k_dram[0], kr[0],
                 waits=[(tsem, scd[q - 1][6])])

    def qp_unit(q, dqc):
        st = qpst[q]
        wsm = wqg if q > 0 else wqw
        prog.dma("gpsimd" if q > 0 else "scalar", wstg[:, 0:1024], wq_l[dqc],
                 wsm, waits=[(wsm, prog.n(wsm)), (vsem, st["wqc"])])
        prog.emit("vector", lambda h: h.tensor_copy(wq_sb, wstg[:, 0:1024]),
                  sem=vsem, waits=[(wsm, prog.n(wsm)),
                                   (tsem, qmm_done.get(q - 1, 0))])
        st["wqc"] = prog.n(vsem)
        for c in range(8):
            prog.emit("tensor", lambda h, c=c: h.matmul(
                pp[6][:], wq_sb[:, 128 * c:128 * (c + 1)],
                xq_sb[:, 512 * c:512 * (c + 1)],
                start=(c == 0), stop=(c == 7)),
                sem=tsem, waits=[(xqw, st["wxq"]),
                                 (vsem, max(st["wqc"],
                                            oproj_ev.get(q - 2, 0)))])
        wt = prog.n(tsem)
        prog.emit("vector", lambda h, dqc=dqc: h.tensor_scalar(
            q_sb[:, 512 * dqc:512 * (dqc + 1)], pp[6][:],
            s_bq[:, dqc:dqc + 1], None, op0=OP.add),
            sem=vsem, waits=[(tsem, wt)])
        wev = prog.n(vsem)
        sqw = [(vsem, wev), (tsem, qst["redmm"])]
        if q == 0 and dqc == 0:
            sqw += [(kwk[0], wpassa_ka), (kwk[1], wpassa_kb)]
        prog.emit("scalar", lambda h, dqc=dqc: h.activation(
            kev, q_sb[:, 512 * dqc:512 * (dqc + 1)], AF.Square),
            sem=ssem, waits=sqw)
        prog.emit("tensor", lambda h: h.matmul(
            pp[7][0:2, :], s_idb[:, 0:2], kev,
            start=True, stop=True), sem=tsem,
            waits=[(ssem, prog.n(ssem)), (vsem, prog.n(vsem))])
        qst["redmm"] = prog.n(tsem)
        prog.emit("vector", lambda h: h.tensor_reduce(
            kred, pp[7][0:2, :], axis=AX.X, op=OP.max),
            sem=vsem, waits=[(tsem, qst["redmm"])])
        prog.emit("vector", lambda h, dqc=dqc: h.tensor_copy(
            qmax[:, dqc:dqc + 1], kred), sem=vsem)
        if dqc == 7:
            qmm_done[q] = prog.n(tsem)
            qevac_all[q] = wev
            wq_copy_last[q] = st["wqc"]
            qmax_done[q] = prog.n(vsem)

    def qp_post(q):
        prog.emit("scalar", lambda h: h.activation(
            qmax, qmax, AF.Sqrt), sem=ssem, waits=[(vsem, qmax_done[q])])
        prog.emit("vector", lambda h: h.tensor_tensor(
            nbias, qmax, kmax, op=OP.mult), sem=vsem,
            waits=[(ssem, prog.n(ssem)), (ssem, wkmax),
                   (nbsm, wnb_q.get(q - 1, 0))])
        prog.emit("vector", lambda h: h.tensor_scalar(
            nbias, nbias, -0.125, -0.0125, op0=OP.mult, op1=OP.add),
            sem=vsem)
        prog.dma("sync", nb_dram.rearrange("o (a b) -> (o a) b", b=8),
                 nbias, nbsm, waits=[(vsem, prog.n(vsem))])
        prog.dma("sync", nbb[q % 2], nb_dram.broadcast_to([128, 16]), nbsm,
                 waits=[(nbsm, prog.n(nbsm)), (ssem, exp_fin.get(q - 2, 0))])
        wnb_q[q] = prog.n(nbsm)

    def mk_tail(qt):
        def pre():
            prog.emit("gpsimd", lambda h: h.dma_start(att[64:128, :],
                                                      g3[0:64, :]),
                      sem=sh, inc=16, waits=[(vsem, wev_q[qt])])
            shift_done[qt] = prog.n(sh)
            for s6 in range(6):
                prog.emit("gpsimd", lambda h, s6=s6: h.indirect_dma_start(
                    v_band[:, 1040 * s6:1040 * (s6 + 1)], None, v_dram,
                    IndirectOffsetOnAxis(ap=s_ivb[:, 6 * qt + s6:
                                               6 * qt + s6 + 1], axis=0)),
                    sem=vbw, inc=16,
                    waits=[(kw, wpassa_k), (dsem, wc1),
                           (tsem, lv_mm_all.get(qt - 1, 0)),
                           (rs2[0], prog.n(rs2[0])),
                           (rs2[1], prog.n(rs2[1]))])
            wvb_q[qt] = prog.n(vbw)
            prog.dma("sync", rs16b,
                     rsraw_dram.rearrange("o (h j) -> (o h) j", j=512), rsm,
                     waits=[(rs2[0], prog.n(rs2[0])), (rs2[1], prog.n(rs2[1])),
                            (vsem, rs_recip.get(qt - 1, 0))])
            prog.emit("vector", lambda h: h.reciprocal(rs16, rs16b),
                      sem=vsem, waits=[(rsm, prog.n(rsm))])
            rs_recip[qt] = prog.n(vsem)
            prog.emit("vector", lambda h: h.tensor_scalar(
                rs16, rs16, 0.6, None, op0=OP.mult), sem=vsem)
            prog.dma("sync", rsf_dram.rearrange("o (h j) -> (o h) j", j=512),
                     rs16, rsm, waits=[(vsem, prog.n(vsem))])
            wrf_q[qt] = prog.n(rsm)

        def lv(u):
            qb, dvc = u % 4, u // 4
            for s3 in range(3):
                b_loc = qb + s3
                vview = (v_band[:, 1040 * b_loc:1040 * (b_loc + 1)]
                         .rearrange("p (h c) -> p h c", c=65)
                         [:, 8 * dvc:8 * (dvc + 1), 0:64])
                prog.emit("tensor", lambda h, qt=qt, qb=qb, s3=s3,
                          vview=vview: h.matmul(
                              pp[6][:],
                              slt_b[qt % 2][:, (3 * qb + s3) * 128:
                                            (3 * qb + s3 + 1) * 128],
                              vview, start=(s3 == 0), stop=(s3 == 2)),
                          sem=tsem, waits=[(vsem, prog.n(vsem)),
                                           (vbw, wvb_q[qt]),
                                           (lts, prog.n(lts))])
            wlv = prog.n(tsem)
            lv_mm_all[qt] = wlv
            for sub in range(4):
                prog.emit("vector", lambda h, sub=sub: h.tensor_copy(
                    s_t[:, 128 * sub:128 * (sub + 1)],
                    pp[6][:, 128 * sub:128 * (sub + 1)]),
                    sem=vsem, waits=[(tsem, wlv)])
            wlve = prog.n(vsem)
            for sub in range(4):
                prog.emit("tensor", lambda h, sub=sub: h.transpose(
                    pp[7][:, 0:128], s_t[:, 128 * sub:128 * (sub + 1)],
                    s_idf[:]), sem=tsem,
                    waits=[(vsem, wlve), (vsem, prog.n(vsem))])
                prog.emit("vector", lambda h, qb=qb, dvc=dvc, sub=sub:
                          h.tensor_copy(
                              g3[:, 512 * (4 * dvc + sub) + 128 * qb:
                                 512 * (4 * dvc + sub) + 128 * (qb + 1)],
                              pp[7][:, 0:128]),
                          sem=vsem, waits=[(tsem, prog.n(tsem)),
                                           (sh, shift_done[qt])])
            wlv_all[qt] = prog.n(vsem)

        def cmb(c):
            ffr = rsf_dram.rearrange("o (h j) -> (o h) j", j=512)
            prog.dma("sync", ff_c[0:64, :],
                     ffr[2 * c:2 * c + 1, :].broadcast_to([64, 512]), rsm,
                     waits=[(rsm, wrf_q[qt]), (vsem, prog.n(vsem))])
            prog.dma("sync", ff_c[64:128, :],
                     ffr[2 * c + 1:2 * c + 2, :].broadcast_to([64, 512]), rsm)
            prog.emit("vector", lambda h, c=c: h.tensor_tensor(
                att[:, 512 * c:512 * (c + 1)], att[:, 512 * c:512 * (c + 1)],
                ff_c, op=OP.mult),
                sem=vsem, waits=[(rsm, prog.n(rsm)), (sh, shift_done[qt])])
            prog.emit("vector", lambda h, c=c: h.tensor_tensor(
                att[:, 512 * c:512 * (c + 1)], att[:, 512 * c:512 * (c + 1)],
                g3[:, 512 * c:512 * (c + 1)], op=OP.add),
                sem=vsem, waits=[(vsem, wlv_all[qt])])
            cmb_done[qt] = prog.n(vsem)

        def opj():
            ocd = {}
            for jc in range(2):
                for qsub in range(4):
                    last = (jc == 1 and qsub == 3)
                    for c in range(8):
                        prog.emit("tensor", lambda h, jc=jc, qsub=qsub, c=c:
                                  h.matmul(
                                      pp[6][:],
                                      att[:, 512 * c + 128 * qsub:
                                          512 * c + 128 * (qsub + 1)],
                                      wo_sb[:, 4096 * jc + 512 * c:
                                            4096 * jc + 512 * (c + 1)],
                                      start=(c == 0), stop=(c == 7)),
                                  sem=tsem,
                                  waits=[(vsem, max(cmb_done[qt], wwo_res)),
                                         (vsem, prog.n(vsem))])
                        if last:
                            ocd[c] = prog.n(tsem)
                    ew = [(tsem, prog.n(tsem))]
                    if jc == 0 and qsub == 0:
                        ew.append((ssem, ln_s_last.get(qt - 1, 0)))
                    prog.emit("vector", lambda h, qsub=qsub, jc=jc:
                              h.scalar_tensor_tensor(
                                  o_tm4[:, 1024 * qsub + 512 * jc:
                                        1024 * qsub + 512 * (jc + 1)],
                                  pp[6][:], 1.0,
                                  bo_bc[:, 512 * jc:512 * (jc + 1)],
                                  op0=OP.mult, op1=OP.add),
                              sem=vsem, waits=ew)
            oproj_cd[qt] = ocd
            oproj_all[qt] = prog.n(tsem)
            oproj_ev[qt] = prog.n(vsem)

        def lnr(qsub):
            tt0 = 4 * qt + qsub
            emit_plain_ln(prog, sems, lnb,
                          o_tm4[:, 1024 * qsub:1024 * (qsub + 1)], o_tm, D,
                          extra_in_waits=[(vsem, prog.n(vsem)),
                                          (osem3, prog.n(osem3))])
            ln_s_last[qt] = prog.n(ssem)
            prog.dma("sync", s_x[0][:], x2_loc[128 * tt0:128 * (tt0 + 1), :],
                     xrs, waits=[(vsem, prog.n(vsem))])
            prog.emit("vector", lambda h: h.tensor_tensor(
                o_tm, o_tm, s_x[0][:], op=OP.add),
                sem=vsem, waits=[(xrs, prog.n(xrs)), (osem3, prog.n(osem3))])
            prog.dma("sync", x3_loc[128 * tt0:128 * (tt0 + 1), :], o_tm,
                     osem3, waits=[(vsem, prog.n(vsem))])
        return pre, lv, cmb, opj, lnr

    def emit_cloop(qt, units):
        for u in units.get("pre", []):
            u()
        exp_done = {}
        av_done = {}
        sc_done = {}
        scd[qt] = sc_done
        sc_mm_done = {}
        ev_done = {}
        efp = exp_fin.get(qt - 1, 0)
        afp = av_fin[qt - 1]
        wnb = wnb_q[qt]
        nbc = nbb[qt % 2]
        for c in range(8):
            ks = ksb[c % 2]
            ksem = kr[c % 2]
            wks = prog.n(kr[c % 2])
            if c + 1 < 8:
                prog.dma("scalar", ksb[(c + 1) % 2], k_dram[c + 1],
                         kr[(c + 1) % 2],
                         waits=[(tsem, sc_done.get(c - 1, prog.n(tsem)))])

            def emit_scores(kb):
                par = kb % 2
                sA = pp[par * 2]
                sB = pp[par * 2 + 1]
                mw = [(ksem, wks),
                      (ssem, exp_done.get((c, kb - 2),
                                          exp_done.get((c - 1, 31), efp)))]
                if c == 0 and kb < 2:
                    mw.append((vsem, qevac_all[qt]))
                prog.emit("tensor", lambda h, sA=sA, ks=ks, kb=kb, c=c:
                          h.matmul(sA[:], ks[0:64, 128 * kb:128 * (kb + 1)],
                                   q_sb[0:64, 512 * c:512 * (c + 1)],
                                   start=True, stop=True),
                          sem=tsem, waits=mw)
                prog.emit("tensor", lambda h, sB=sB, ks=ks, kb=kb, c=c:
                          h.matmul(sB[:], ks[64:128, 128 * kb:128 * (kb + 1)],
                                   q_sb[64:128, 512 * c:512 * (c + 1)],
                                   start=True, stop=True,
                                   tile_position=(64, 0)),
                          sem=tsem)
                sc_mm_done[(c, kb)] = prog.n(tsem)

            emit_scores(0)
            for kb in range(32):
                par = kb % 2
                sA = pp[par * 2]
                sB = pp[par * 2 + 1]
                eA = expt[par * 2]
                eB = expt[par * 2 + 1]
                ew = [(tsem, sc_mm_done[(c, kb)]), (nbsm, wnb),
                      (tsem, av_done.get((c, kb - 2), afp))]
                prog.emit("scalar", lambda h, sA=sA, c=c, eA=eA, nbc=nbc:
                          h.activation(
                              eA, sA[:], AF.Exp, scale=0.125,
                              bias=nbc[:, c:c + 1]),
                          sem=ssem, waits=ew)
                prog.emit("scalar", lambda h, sB=sB, c=c, eB=eB, nbc=nbc:
                          h.activation(
                              eB, sB[:], AF.Exp, scale=0.125,
                              bias=nbc[:, 8 + c:8 + c + 1]),
                          sem=ssem)
                we = prog.n(ssem)
                exp_done[(c, kb)] = we
                if kb + 1 < 32:
                    emit_scores(kb + 1)
                avw = [(ssem, we)]
                if kb == 0:
                    avw.append((vsem, ev_done.get(c - 1,
                                                  wev_q.get(qt - 1, 0))))
                prog.emit("tensor", lambda h, kb=kb, c=c, eA=eA:
                          h.matmul(
                              avA[0:65, :],
                              vtm[:, 1040 * kb + 130 * c:
                                  1040 * kb + 130 * c + 65],
                              eA, start=(kb == 0), stop=(kb == 31)),
                          sem=tsem, waits=avw)
                prog.emit("tensor", lambda h, kb=kb, c=c, eB=eB:
                          h.matmul(
                              avB[0:65, :],
                              vtm[:, 1040 * kb + 130 * c + 65:
                                  1040 * kb + 130 * c + 130],
                              eB, start=(kb == 0), stop=(kb == 31)),
                          sem=tsem)
                av_done[(c, kb)] = prog.n(tsem)
                for u in units.get((c, kb), []):
                    u()
            wav = prog.n(tsem)
            sc_done[c] = wav
            for u in units.get((c, 32), []):
                u()
            rw = [(tsem, wav), (rs2[0], prog.n(rs2[0])),
                  (rs2[1], prog.n(rs2[1])),
                  (tsem, lv_mm_all.get(qt - 1, 0)),
                  (vbw, wvb_q.get(qt - 1, 0))]
            prog.emit("vector", lambda h: h.tensor_copy(
                v_band[64:65, 0:512], avA[64:65, :]),
                sem=vsem, waits=rw)
            prog.emit("vector", lambda h: h.tensor_copy(
                v_band[64:65, 512:1024], avB[64:65, :]),
                sem=vsem)
            wrc = prog.n(vsem)
            prog.dma("sync", rsraw_dram[0:1, 1024 * c:1024 * c + 512],
                     v_band[64:65, 0:512], rs2[0], waits=[(vsem, wrc)])
            prog.dma("sync", rsraw_dram[0:1, 1024 * c + 512:1024 * (c + 1)],
                     v_band[64:65, 512:1024], rs2[1], waits=[(vsem, wrc)])
            evw = [(tsem, wav), (sh, shift_done.get(qt - 1, 0)),
                   (tsem, oproj_cd.get(qt - 1, {}).get(c, 0))]
            if qt == 0 and c == 0:
                evw += [(xqs[0], wpassa_xa), (xqs[1], wpassa_xb)]
            prog.emit("vector", lambda h, c=c: h.tensor_copy(
                att[0:64, 512 * c:512 * (c + 1)], avA[0:64, :]),
                sem=vsem, waits=evw)
            prog.emit("vector", lambda h, c=c: h.tensor_copy(
                g3[0:64, 512 * c:512 * (c + 1)], avB[0:64, :]),
                sem=vsem)
            ev_done[c] = prog.n(vsem)
        wev_q[qt] = prog.n(vsem)
        exp_fin[qt] = prog.n(ssem)
        av_fin[qt] = sc_done[7]
        sc6[qt] = sc_done[6]
        scmm31[qt] = sc_mm_done[(7, 31)]

    av_fin[-1] = prog.n(tsem)
    emit_qproj(0)
    for qt in range(4):
        units = {}

        def add(key, fn):
            units.setdefault(key, []).append(fn)

        if qt >= 1:
            pre, lv, cmb, opj, lnr = mk_tail(qt - 1)
            add("pre", pre)
            for u8 in range(8):
                add((0, 1 + 2 * u8), lambda u8=u8, lv=lv: lv(u8))
            for c8 in range(8):
                add((0, 17 + c8), lambda c8=c8, cmb=cmb: cmb(c8))
            add((0, 32), opj)
            for qs in range(4):
                add((1, 4 + 5 * qs), lambda qs=qs, lnr=lnr: lnr(qs))
        if qt < 3:
            q = qt + 1
            add("pre", lambda q=q: qp_pre(q))
            add((0, 27), lambda q=q: qp_lt(q))
            for dqc in range(7):
                add((dqc + 1, 26), lambda q=q, dqc=dqc: qp_unit(q, dqc))
            add((7, 10), lambda q=q: qp_ksb(q))
            add((7, 30), lambda q=q: qp_unit(q, 7))
        emit_cloop(qt, units)
        if qt < 3:
            qp_post(qt + 1)
    pre, lv, cmb, opj, lnr = mk_tail(3)
    pre()
    for u8 in range(8):
        lv(u8)
    for c8 in range(8):
        cmb(c8)
    opj()
    for qs in range(4):
        lnr(qs)
    barrier(prog)
    ar.reset()

    dsem = sem("p4d"); vsem = sem("p4v"); ssem = sem("p4s"); tsem = sem("p4t")
    wsem4 = sem("p4w")
    xs4 = [sem("p4x0"), sem("p4x1")]
    os4 = [sem("p4o0"), sem("p4o1")]
    sems = dict(vsem=vsem, ssem=ssem)
    prog.bind("vector", vsem); prog.bind("scalar", ssem)

    wstg4 = ar.al(D, F32)

    def mk_qb():
        d = dict(zt=ar.al(D, F32), i32t=ar.al(D, I32),
                 i_f=ar.al(D, F32), qs=ar.al(D, F32),
                 qb=ar.al(D, F32), az=ar.al(D, F32),
                 msk=ar.al(D, U32), q=ar.al(D, F32),
                 f8t=ar.al(D // 2, BF16).bitcast(mybir.dt.float8e4))
        d["wmax"] = ar.al(32, F32)
        d["scl"] = ar.al(32, F32)
        d["rcp"] = ar.al(32, F32)
        return d
    qb2 = [mk_qb(), mk_qb()]
    qb_ = qb2[0]
    wd_c = ar.al(D, BF16)
    wdT = [ar.al(D, BF16) for _ in range(8)]
    xd_t2 = [ar.al(D, BF16) for _ in range(2)]
    xq_fm2 = [ar.al(D, BF16) for _ in range(2)]
    ffn4_2 = [ar.al(D, F32) for _ in range(2)]
    z4_2 = [ar.al(D, F32), s_z[0:128, 0:D]]
    st4 = {k: ar.al(1, F32) for k in ("ssum", "ssq", "sm", "sv", "sr")}
    lnb4 = dict(sq=s_t[:, 0:D], ssum=st4["ssum"], ssq=st4["ssq"],
                sm=st4["sm"], sv=st4["sv"], sr=st4["sr"], eps=lnb["eps"])
    lnb4_2 = [lnb4, lnb]
    bias_bc = ar.al(D, F32)

    prog.dma("sync", bias_bc, b4_in.broadcast_to([128, D]), dsem)
    wc4 = prog.n(dsem)

    for oc in range(8):
        prog.dma("sync", wstg4, w4_in[128 * oc:128 * (oc + 1), :], wsem4,
                 waits=[(vsem, prog.n(vsem)), (ssem, prog.n(ssem))])
        wq_ = emit_quant(prog, sems, qb_, wstg4, wd_c,
                         waits0=[(wsem4, prog.n(wsem4))])
        for ic in range(8):
            tw = [(vsem, wq_), (dsem, wc4)]
            if oc * 8 + ic >= 2:
                tw.append((vsem, prog.n(vsem)))
            prog.emit("tensor", lambda h, ic=ic: h.transpose(
                pp[ic % 2][:, 0:128].bitcast(BF16)[:, 0:128],
                wd_c[:, 128 * ic:128 * (ic + 1)], s_idb[:]),
                sem=tsem, waits=tw)
            prog.emit("vector", lambda h, ic=ic, oc=oc: h.tensor_copy(
                wdT[ic][:, 128 * oc:128 * (oc + 1)],
                pp[ic % 2][:, 0:128].bitcast(BF16)[:, 0:128]),
                sem=vsem, waits=[(tsem, prog.n(tsem))])
    wwd = prog.n(vsem)

    xq_ready = {}
    resid_done = {}
    ffn_mm_done = {}
    ffn_evac_done = {}
    out_dma_done = {}
    for tt in range(16):
        par = tt % 2
        buf = s_x[par]
        xsem = xs4[par]
        xd_t = xd_t2[par]
        xq_fm = xq_fm2[par]
        ffn4 = ffn4_2[par]
        z4 = z4_2[par]
        qbp = qb2[par]
        lnp = lnb4_2[par]
        w = [(dsem, wc4)]
        if tt >= 2:
            w = [(vsem, resid_done[tt - 2])]
        prog.dma("sync", buf[:], x3_loc[128 * tt:128 * (tt + 1), :], xsem,
                 waits=w)
        wd = prog.n(xsem)
        emit_plain_ln(prog, sems, lnp, buf[:], z4, D,
                      extra_in_waits=[(xsem, wd),
                                      (tsem, ffn_mm_done.get(tt - 2, 0))])
        emit_quant(prog, sems, qbp, z4, xd_t,
                   waits0=[(vsem, prog.n(vsem))])
        wq_ = prog.n(vsem)
        for ic in range(8):
            tw = [(vsem, wq_), (vsem, prog.n(vsem))]
            prog.emit("tensor", lambda h, ic=ic, xd_t=xd_t: h.transpose(
                pp[ic % 2][:, 0:128].bitcast(BF16)[:, 0:128],
                xd_t[:, 128 * ic:128 * (ic + 1)], s_idb[:]),
                sem=tsem, waits=tw)
            prog.emit("vector", lambda h, ic=ic, xq_fm=xq_fm: h.tensor_copy(
                xq_fm[:, 128 * ic:128 * (ic + 1)],
                pp[ic % 2][:, 0:128].bitcast(BF16)[:, 0:128]),
                sem=vsem, waits=[(tsem, prog.n(tsem))])
        wxq = prog.n(vsem)
        xq_ready[tt] = wxq
        for jc in range(2):
            bank = pp[4 + 2 * (par) + jc - jc] if False else pp[4 + jc]
            for ic in range(8):
                prog.emit("tensor", lambda h, jc=jc, ic=ic, xq_fm=xq_fm:
                          h.matmul(
                              pp[4 + jc][:], xq_fm[:, 128 * ic:128 * (ic + 1)],
                              wdT[ic][:, 512 * jc:512 * (jc + 1)],
                              start=(ic == 0), stop=(ic == 7)),
                          sem=tsem,
                          waits=[(vsem, max(wxq, wwd)),
                                 (vsem, ffn_evac_done.get(tt - 1, 0))])
            prog.emit("vector", lambda h, jc=jc, ffn4=ffn4:
                      h.scalar_tensor_tensor(
                          ffn4[:, 512 * jc:512 * (jc + 1)], pp[4 + jc][:], 1.0,
                          bias_bc[:, 512 * jc:512 * (jc + 1)],
                          op0=OP.mult, op1=OP.add),
                      sem=vsem,
                      waits=[(tsem, prog.n(tsem)),
                             out_dma_done.get(tt - 2, (dsem, 0))])
        ffn_evac_done[tt] = prog.n(vsem)
        ffn_mm_done[tt] = prog.n(tsem)
        emit_plain_ln(prog, sems, lnp, ffn4, ffn4, D,
                      extra_in_waits=[(vsem, prog.n(vsem)),
                                      out_dma_done.get(tt - 2, (dsem, 0))])
        prog.emit("vector", lambda h, buf=buf, ffn4=ffn4: h.tensor_tensor(
            ffn4, ffn4, buf[:], op=OP.add),
            sem=vsem, waits=[out_dma_done.get(tt - 2, (dsem, 0))])
        resid_done[tt] = prog.n(vsem)
        prog.dma("sync", out_d[128 * tt:128 * (tt + 1), :], ffn4, os4[par],
                 waits=[(vsem, prog.n(vsem))])
        out_dma_done[tt] = (os4[par], prog.n(os4[par]))
    prog.wait("sync", (os4[0], prog.n(os4[0])), (os4[1], prog.n(os4[1])))


def build_lt_blocks_v2(th):
    S = 4096
    out = np.zeros((16, 3, 128, 128), dtype=np.float64)

    def lrow_val(q, k):
        d = q - k
        if abs(d) > 3 or k < 0 or k >= S:
            return 0.0
        num = np.exp(-(d * d) / (2.0 * 0.4 ** 2))
        den = 1e-10
        for j in range(max(0, q - 3), min(S - 1, q + 3) + 1):
            dd = q - j
            den += np.exp(-(dd * dd) / (2.0 * 0.4 ** 2))
        return num / den

    for qb in range(16):
        for s3 in range(3):
            gk = 16 * th + qb - 1 + s3
            if gk < 0 or gk > 31:
                continue
            for ql in range(128):
                q_true = th * 2048 + qb * 128 + ql
                for dk in range(-3, 4):
                    k_true = q_true + dk
                    kl = k_true - 128 * gk
                    if 0 <= kl < 128:
                        out[qb, s3, kl, ql] = 0.4 * lrow_val(q_true, k_true)
    return out


def prep_fused_inputs(inputs, core):
    b, th = core // 2, core % 2
    d1 = prep_launch1_inputs(inputs, core)
    f32 = np.float32
    x_res = np.ascontiguousarray(
        np.asarray(inputs["L0_attn_x"])[b, 2048 * th:2048 * (th + 1)]).astype(f32)
    p = np.arange(128)
    idx_y = np.zeros((128, 32), np.int32)
    for tt in range(16):
        for blk in range(2):
            idx_y[:, 2 * tt + blk] = blk * 4096 + 2048 * th + 128 * tt + p
    idx_vb = np.zeros((128, 24), np.int32)
    for qt in range(4):
        for s6 in range(6):
            gb = min(max(16 * th + 4 * qt - 1 + s6, 0), 31)
            idx_vb[:, 6 * qt + s6] = 128 * gb + p
    d3 = prep_launch3_inputs(inputs, core, [np.zeros((4096, 1024), f32)] * 4)
    lt = build_lt_blocks_v2(th)
    out = dict(d1)
    out.update(dict(
        x_res=x_res,
        lnab=np.stack([np.asarray(inputs["L0_attn_ln_a"]),
                       np.asarray(inputs["L0_attn_ln_b"])]).astype(f32),
        theta=np.asarray(inputs["L0_ffn_theta"]).astype(f32),
        thn=np.asarray(inputs["L0_ffn_theta_noise"]).astype(f32),
        phi=np.asarray(inputs["L0_ffn_phi"]).astype(f32),
        phn=np.asarray(inputs["L0_ffn_phi_noise"]).astype(f32),
        idx_y=idx_y,
        wk_l=d3["wk_l"], wv_l=d3["wv_l"], wq_l=d3["wq_l"], wo_l=d3["wo_l"],
        bk=d3["bk"], bq=d3["bq"], bvaug=d3["bvaug"], bo=d3["bo"],
        lt=np.ascontiguousarray(
            lt.transpose(2, 0, 1, 3).reshape(128, 16 * 3 * 128)).astype(f32),
        idx_vb=idx_vb,
        weight=np.asarray(inputs["L1_ffn_weight"]).astype(f32),
        bias=np.asarray(inputs["L1_ffn_bias"]).reshape(1, 1024).astype(f32),
    ))
    return out


from concourse.bass_utils import run_bass_kernel_spmd

_NCF = []


def kernel(**inputs):
    inputs = {k: np.asarray(v) for k, v in inputs.items()}
    if not _NCF:
        _NCF.append(build_fused())
    cores = list(range(8))
    res = run_bass_kernel_spmd(
        _NCF[0], [prep_fused_inputs(inputs, c) for c in cores], core_ids=cores)
    out = np.stack([np.concatenate([res.results[2 * b]["out_rows"],
                                    res.results[2 * b + 1]["out_rows"]], axis=0)
                    for b in range(4)])
    return out.astype(np.float32)


# version 19 (final)
# speedup vs baseline: 1.0077x; 1.0077x over previous
import numpy as np
import ml_dtypes
import concourse.bass as bass
import concourse.mybir as mybir
from concourse.bass import IndirectOffsetOnAxis
from contextlib import ExitStack

F32 = mybir.dt.float32
BF16 = mybir.dt.bfloat16
I32 = mybir.dt.int32
U32 = mybir.dt.uint32
AF = mybir.ActivationFunctionType
OP = mybir.AluOpType
AX = mybir.AxisListType

ENGS = ("sync", "vector", "scalar", "tensor", "gpsimd")


class Sem:
    __slots__ = ("h", "name")

    def __init__(self, h, name):
        self.h = h
        self.name = name


class Prog:
    def __init__(self, nc):
        self.nc = nc
        self.streams = {e: [] for e in ENGS}
        self.counts = {}
        self.eng_sem = {}

    def bind(self, eng, sem):
        self.eng_sem[eng] = sem

    def sem(self, ctx, name):
        h = ctx.enter_context(self.nc.semaphore(name))
        self.counts[name] = 0
        return Sem(h, name)

    def emit(self, eng, builder, sem=None, inc=1, waits=()):
        waits = tuple(waits)
        own = self.eng_sem.get(eng)
        if own is not None:
            c = self.counts[own.name]
            if c > 0:
                waits = waits + ((own, c),)
        if sem is not None:
            self.counts[sem.name] += inc

        best = {}
        for s, c in waits:
            if c > 0 and (s.name not in best or best[s.name][1] < c):
                best[s.name] = (s, c)
        waits = tuple(best.values())

        def f(h):
            for s, c in waits:
                h.wait_ge(s.h, c)
            ins = builder(h)
            if sem is not None:
                ins.then_inc(sem.h, inc)
        self.streams[eng].append(f)

    def dma(self, eng, out, in_, sem, waits=()):
        self.emit(eng, lambda h, o=out, i=in_: h.dma_start(o, i),
                  sem=sem, inc=16, waits=waits)

    def wait(self, eng, *pairs):
        pairs = tuple(pairs)

        def f(h):
            for s, c in pairs:
                h.wait_ge(s.h, c)
        self.streams[eng].append(f)

    def n(self, sem):
        return self.counts[sem.name]

    def run(self, block):
        for eng in ENGS:
            fns = self.streams[eng]
            if not fns:
                continue

            def body(h, fns=fns):
                for f in fns:
                    f(h)
            getattr(block, eng)(body)



def emit_plain_ln(prog, sems, bufs, x_in, out, D, extra_in_waits=()):
    vsem, ssem = sems["vsem"], sems["ssem"]
    prog.emit("scalar", lambda h: h.activation(
        bufs["sq"], x_in, AF.Square),
        sem=ssem, waits=extra_in_waits)
    ws = prog.n(ssem)
    prog.emit("vector", lambda h: h.tensor_reduce(
        bufs["ssum"], x_in.rearrange("p (a b) -> p a b", b=32),
        axis=AX.XY, op=OP.add), sem=vsem, waits=extra_in_waits)
    prog.emit("vector", lambda h: h.tensor_reduce(
        bufs["ssq"], bufs["sq"].rearrange("p (a b) -> p a b", b=32),
        axis=AX.XY, op=OP.add), sem=vsem, waits=[(ssem, ws)])
    prog.emit("vector", lambda h: h.tensor_scalar(
        bufs["sm"], bufs["ssum"], -1.0 / D, None, op0=OP.mult), sem=vsem)
    prog.emit("vector", lambda h: h.tensor_tensor(
        bufs["sv"], bufs["sm"], bufs["sm"], op=OP.mult), sem=vsem)
    prog.emit("vector", lambda h: h.scalar_tensor_tensor(
        bufs["sv"], bufs["ssq"], 1.0 / D, bufs["sv"],
        op0=OP.mult, op1=OP.subtract), sem=vsem, waits=[(ssem, ws)])
    prog.emit("scalar", lambda h: h.activation(
        bufs["sr"], bufs["sv"], AF.Sqrt, bias=bufs["eps"]),
        sem=ssem, waits=[(vsem, prog.n(vsem))])
    prog.emit("vector", lambda h: h.reciprocal(bufs["sr"], bufs["sr"]),
              sem=vsem, waits=[(ssem, prog.n(ssem))])
    prog.emit("vector", lambda h: h.tensor_scalar(
        out, x_in, bufs["sm"], bufs["sr"], op0=OP.add, op1=OP.mult), sem=vsem)
    return prog.n(vsem), prog.n(ssem)


def emit_quant(prog, sems, bufs, x_in, out_bf, waits0=()):
    vsem, ssem = sems["vsem"], sems["ssem"]
    OPl = OP
    g = lambda ap: ap.rearrange("p (g e) -> p g e", e=32)
    prog.emit("vector", lambda h: h.tensor_reduce(
        bufs["wmax"], g(x_in), axis=AX.X, op=OPl.max,
        apply_absolute_value=True), sem=vsem, waits=tuple(waits0))
    prog.emit("vector", lambda h: h.tensor_scalar(
        bufs["scl"], bufs["wmax"], 1.0 / 7.5, 1e-30,
        op0=OPl.mult, op1=OPl.max), sem=vsem)
    prog.emit("vector", lambda h: h.reciprocal(bufs["rcp"], bufs["scl"]),
              sem=vsem)
    prog.emit("vector", lambda h: h.tensor_tensor(
        g(bufs["zt"]), g(x_in),
        bufs["rcp"].unsqueeze(2).broadcast_to([128, 32, 32]),
        op=OPl.mult), sem=vsem)
    prog.emit("scalar", lambda h: h.activation(
        bufs["i32t"], bufs["zt"], AF.Copy, scale=8.0),
        sem=ssem, waits=[(vsem, prog.n(vsem))])
    ws1 = prog.n(ssem)
    prog.emit("scalar", lambda h: h.activation(
        bufs["f8t"], bufs["zt"], AF.Copy), sem=ssem)
    ws2 = prog.n(ssem)
    prog.emit("scalar", lambda h: h.activation(
        bufs["az"], bufs["zt"], AF.Abs), sem=ssem)
    ws = prog.n(ssem)
    prog.emit("vector", lambda h: h.tensor_copy(bufs["i_f"], bufs["i32t"]),
              sem=vsem, waits=[(ssem, ws1)])
    prog.emit("vector", lambda h: h.tensor_scalar(
        bufs["qs"], bufs["i_f"], 0.125, None, op0=OPl.mult), sem=vsem)
    prog.emit("vector", lambda h: h.tensor_copy(bufs["qb"], bufs["f8t"]),
              sem=vsem, waits=[(ssem, ws2)])
    prog.emit("vector", lambda h: h.tensor_scalar(
        bufs["msk"], bufs["az"], 2.0, None, op0=OPl.is_lt), sem=vsem,
        waits=[(ssem, ws)])
    prog.emit("vector", lambda h: h.select(
        bufs["q"], bufs["msk"], bufs["qs"], bufs["qb"], add_drain=True),
        sem=vsem)
    prog.emit("vector", lambda h: h.tensor_tensor(
        g(out_bf), g(bufs["q"]),
        bufs["scl"].unsqueeze(2).broadcast_to([128, 32, 32]),
        op=OPl.mult), sem=vsem)
    return prog.n(vsem)


def prep_launch1_inputs(inputs, core):
    b, half = core // 2, core % 2
    x = np.asarray(inputs["L0_attn_x"][b])
    xp = np.asarray(inputs["L0_attn_x_proj_w"])
    dtw = np.asarray(inputs["L0_attn_dt_proj_w"])
    dtb = np.asarray(inputs["L0_attn_dt_proj_b"])
    alog = np.asarray(inputs["L0_attn_A_log"])
    dpar = np.asarray(inputs["L0_attn_D_param"])
    lo, hi = (0, 512) if half == 0 else (512, 1024)
    perm = np.r_[0:1024] if half == 0 else np.r_[512:1024, 0:512]
    xperm = np.ascontiguousarray(x[:, perm])
    xpT = np.ascontiguousarray(xp[:, perm].T)
    sl = slice(lo, hi)
    ident = np.eye(128)
    return dict(
        x=xperm.astype(np.float32),
        xprojT=np.ascontiguousarray(
            xpT.reshape(8, 128, 96).transpose(1, 0, 2).reshape(128, 768)
        ).astype(np.float32),
        dtwT=np.ascontiguousarray(dtw[sl].T).astype(np.float32),
        dtb=np.ascontiguousarray(dtb[sl].reshape(4, 128).T).astype(np.float32),
        alog=np.ascontiguousarray(
            alog[sl].reshape(4, 128, 16).transpose(1, 0, 2).reshape(128, 64)
        ).astype(np.float32),
        dpar=np.ascontiguousarray(dpar[sl].reshape(4, 128).T).astype(np.float32),
        identf=ident.astype(np.float32),
        identb=ident.astype(ml_dtypes.bfloat16),
    )


def prep_launch3_inputs(inputs, core, x2_full):
    b, th = core // 2, core % 2
    own = slice(2048 * th, 2048 * (th + 1))
    oth = slice(2048 * (1 - th), 2048 * (2 - th))
    x2 = np.asarray(x2_full[b])
    x2c = np.concatenate([x2[own], x2[oth]], 0)
    Wq = np.asarray(inputs["L1_attn_Wq"]); Wk = np.asarray(inputs["L1_attn_Wk"])
    Wv = np.asarray(inputs["L1_attn_Wv"]); Wo = np.asarray(inputs["L1_attn_Wo"])
    wqT = Wq.T; wkT = Wk.T; wvT = Wv.T; woT = Wo.T
    wvaT = np.zeros((1024, 1040), dtype=np.float64)
    for hh in range(16):
        wvaT[:, 65 * hh:65 * hh + 64] = wvT[:, 64 * hh:64 * (hh + 1)]
    bva = np.zeros(1040); bva[64::65] = 1.0
    bv = np.asarray(inputs["L1_attn_bv"])
    for hh in range(16):
        bva[65 * hh:65 * hh + 64] = bv[64 * hh:64 * (hh + 1)]
    ident = np.eye(128)
    f32 = np.float32
    return dict(
        x2full=np.ascontiguousarray(x2c).astype(f32),
        wk_l=np.ascontiguousarray(
            wkT.reshape(8, 128, 8, 128).transpose(1, 2, 0, 3).reshape(128, 8192)
        ).astype(f32),
        wv_l=np.ascontiguousarray(
            wvaT.reshape(8, 128, 1040).transpose(1, 0, 2).reshape(128, 8320)
        ).astype(f32),
        wq_l=np.ascontiguousarray(
            wqT.reshape(8, 128, 8, 128).transpose(2, 1, 0, 3).reshape(8, 128, 1024)
        ).astype(f32),
        wo_l=np.ascontiguousarray(
            woT.reshape(8, 128, 2, 512).transpose(2, 1, 0, 3).reshape(2, 128, 4096)
        ).astype(f32),
        bk=np.ascontiguousarray(
            np.asarray(inputs["L1_attn_bk"]).reshape(8, 128).T).astype(f32),
        bq=np.ascontiguousarray(
            np.asarray(inputs["L1_attn_bq"]).reshape(8, 128).T).astype(f32),
        bvaug=bva.reshape(1, 1040).astype(f32),
        bo=np.asarray(inputs["L1_attn_bo"]).reshape(1, 1024).astype(f32),
        identf=ident.astype(f32),
        identb=ident.astype(ml_dtypes.bfloat16),
    )




T, D, R, N = 4096, 1024, 64, 16
DH, NDC = 512, 4


class Arena:
    def __init__(self, handle, cols):
        self.h = handle
        self.cols = cols
        self.off = 0

    def al(self, cols, dtype=F32):
        w = {2: 1, 4: 2}[mybir.dt.size(dtype)]
        need = cols * w
        off = (self.off + 31) & ~31
        assert off + need <= self.cols, f"arena overflow {off}+{need}>{self.cols}"
        self.off = off + need
        view = self.h[:, off:off + need]
        if dtype != BF16:
            view = view.bitcast(dtype)
        return view

    def reset(self):
        self.off = 0


def barrier(prog, engs=("sync", "vector", "scalar", "tensor", "gpsimd"),
            skip=()):
    pairs = tuple((s, c) for s, c in
                  ((prog._sems[n], prog.counts[n]) for n in prog._sems
                   if n not in skip)
                  if c > 0)
    for e in engs:
        prog.wait(e, *pairs)


def build_fused():
    nc = bass.Bass("TRN2", target_bir_lowering=False)
    x_in = nc.dram_tensor("x", [T, D], F32, kind="ExternalInput").ap()
    xprojT = nc.dram_tensor("xprojT", [128, 8 * 96], F32, kind="ExternalInput").ap()
    dtwT = nc.dram_tensor("dtwT", [R, DH], F32, kind="ExternalInput").ap()
    dtb = nc.dram_tensor("dtb", [128, NDC], F32, kind="ExternalInput").ap()
    alog = nc.dram_tensor("alog", [128, NDC * N], F32, kind="ExternalInput").ap()
    dpar = nc.dram_tensor("dpar", [128, NDC], F32, kind="ExternalInput").ap()
    identf_d = nc.dram_tensor("identf", [128, 128], F32, kind="ExternalInput").ap()
    identb_d = nc.dram_tensor("identb", [128, 128], BF16, kind="ExternalInput").ap()
    x_res = nc.dram_tensor("x_res", [2048, D], F32, kind="ExternalInput").ap()
    lnab = nc.dram_tensor("lnab", [2, D], F32, kind="ExternalInput").ap()
    theta = nc.dram_tensor("theta", [D, D], F32, kind="ExternalInput").ap()
    thn = nc.dram_tensor("thn", [D, D], F32, kind="ExternalInput").ap()
    phi = nc.dram_tensor("phi", [D, D], F32, kind="ExternalInput").ap()
    phn = nc.dram_tensor("phn", [D, D], F32, kind="ExternalInput").ap()
    idx_y = nc.dram_tensor("idx_y", [128, 32], I32, kind="ExternalInput").ap()
    wk_l = nc.dram_tensor("wk_l", [128, 8192], F32, kind="ExternalInput").ap()
    wv_l = nc.dram_tensor("wv_l", [128, 8320], F32, kind="ExternalInput").ap()
    wq_l = nc.dram_tensor("wq_l", [8, 128, 1024], F32, kind="ExternalInput").ap()
    wo_l = nc.dram_tensor("wo_l", [2, 128, 4096], F32, kind="ExternalInput").ap()
    bk_in = nc.dram_tensor("bk", [128, 8], F32, kind="ExternalInput").ap()
    bq_in = nc.dram_tensor("bq", [128, 8], F32, kind="ExternalInput").ap()
    bv_in = nc.dram_tensor("bvaug", [1, 1040], F32, kind="ExternalInput").ap()
    bo_in = nc.dram_tensor("bo", [1, 1024], F32, kind="ExternalInput").ap()
    lt_in = nc.dram_tensor("lt", [128, 16 * 3 * 128], F32, kind="ExternalInput").ap()
    idx_vb = nc.dram_tensor("idx_vb", [128, 4 * 6], I32, kind="ExternalInput").ap()
    w4_in = nc.dram_tensor("weight", [D, D], F32, kind="ExternalInput").ap()
    b4_in = nc.dram_tensor("bias", [1, D], F32, kind="ExternalInput").ap()
    out_d = nc.dram_tensor("out_rows", [2048, D], F32, kind="ExternalOutput").ap()
    bc_dram = nc.dram_tensor("bc_dram", [32, T], BF16).ap()
    y_loc = nc.dram_tensor("y_loc", [T, DH], F32, kind="Internal").ap()
    y_cat = nc.dram_tensor("y_cat", [2 * T, DH], F32, kind="Internal").ap()
    x2_loc = nc.dram_tensor("x2_loc", [2048, D], F32, kind="Internal").ap()
    x2cat = nc.dram_tensor("x2cat", [T, D], F32, kind="Internal").ap()
    k_dram = nc.dram_tensor("k_dram", [8, 128, T], BF16).ap()
    xfm_dram = nc.dram_tensor("xfm_dram", [8, 128, 2048], BF16).ap()
    v_dram = nc.dram_tensor("v_dram", [T, 1040], BF16).ap()
    rsraw_dram = nc.dram_tensor("rsraw_dram", [1, 16 * 512], BF16).ap()
    rsf_dram = nc.dram_tensor("rsf_dram", [1, 16 * 512], F32).ap()
    nb_dram = nc.dram_tensor("nb_dram", [1, 16], F32).ap()
    x3_loc = nc.dram_tensor("x3_loc", [2048, D], F32, kind="Internal").ap()

    with ExitStack() as ctx:
        def sbt(name, shape, dtype=F32):
            return ctx.enter_context(nc.sbuf_tensor(name, shape, dtype))

        s_idf = sbt("s_idf", [128, 128], F32)
        s_idb = sbt("s_idb", [128, 128], BF16)
        s_x = [sbt(f"s_x{i}", [128, D]) for i in range(2)]
        s_z = sbt("s_z", [128, D])
        s_t = sbt("s_t", [128, D])
        st = {k: sbt("st_" + k, [128, 1], F32) for k in
              ("ssum", "ssq", "sm", "sv", "sr", "eps", "e12", "pih")}
        lnb = dict(sq=s_t[:, 0:D], ssum=st["ssum"][:], ssq=st["ssq"][:],
                   sm=st["sm"][:], sv=st["sv"][:], sr=st["sr"][:],
                   eps=st["eps"][:])
        ACOLS = 92928
        ar = Arena(sbt("arena", [128, ACOLS], BF16), ACOLS)
        pp = [ctx.enter_context(nc.psum_tensor(f"pp{i}", [128, 512], F32))
              for i in range(8)]

        prog = Prog(nc)
        prog._sems = {}

        def sem(name):
            s = prog.sem(ctx, name)
            prog._sems[name] = s
            return s

        dsem = sem("p1d"); vsem = sem("p1v"); ssem = sem("p1s"); tsem = sem("p1t")
        bsem2 = [sem("p1b0"), sem("p1b1")]; bcsem = sem("p1bc")
        gsem = sem("p1g")
        xs = [sem("p1x0"), sem("p1x1")]
        osm = [sem("p1o0"), sem("p1o1")]
        sems = dict(vsem=vsem, ssem=ssem)
        prog.bind("vector", vsem); prog.bind("scalar", ssem)

        x_bf = [ar.al(T, BF16) for _ in range(8)]
        s_xpj_all = ar.al(8 * 96, BF16)
        _dbc_reg = ar.al(T, BF16)
        deltar_bf = _dbc_reg
        bcpad_bf = _dbc_reg
        s_dtw_h = sbt("p1_dtw", [64, DH], F32)
        s_dtw = s_dtw_h[:]
        s_dtw_bf_h = sbt("p1_dtwb", [64, DH], BF16)
        s_dtw_bf = s_dtw_bf_h[:]
        s_dtb = ar.al(NDC, F32)
        s_alog = ar.al(NDC * N, F32)
        s_aneg = ar.al(NDC * N, F32)
        s_dpar = ar.al(NDC, F32)
        delta_dc = ar.al(T, F32)
        s_xpjf = delta_dc[:, 0:768]
        sp_t1 = ar.al(T, F32)
        s_xln = sp_t1[:, 0:D]
        lnb1 = dict(lnb); lnb1["sq"] = sp_t1[:, D:2 * D]
        dx_dc = ar.al(T, BF16)
        dA0 = ar.al(T, F32)
        h_t = ar.al(T, BF16)
        bxt0 = ar.al(T, BF16)
        hc0 = ar.al(T, BF16)
        b_rep0 = ar.al(T, BF16)
        c_rep0 = ar.al(T, BF16)
        y_stage = ar.al(512, F32)
        ytm = [ar.al(512, F32) for _ in range(2)]

        prog.dma("sync", s_idf[:], identf_d, dsem)
        prog.dma("sync", s_idb[:], identb_d, dsem)
        prog.dma("sync", s_dtw, dtwT, dsem)
        prog.dma("sync", s_dtb, dtb, dsem)
        prog.dma("sync", s_dpar, dpar, dsem)
        prog.dma("sync", s_alog, alog, dsem)
        prog.dma("sync", s_xpjf, xprojT, dsem)
        wconst = prog.n(dsem)
        prog.emit("vector", lambda h: h.memset(st["eps"][:], 1e-5), sem=vsem)
        prog.emit("vector", lambda h: h.memset(st["e12"][:], 1e-12), sem=vsem)
        prog.emit("vector", lambda h: h.memset(st["pih"][:], np.pi / 2), sem=vsem)
        prog.emit("vector", lambda h: h.tensor_copy(s_dtw_bf, s_dtw),
                  sem=vsem, waits=[(dsem, wconst)])
        prog.emit("scalar", lambda h: h.activation(
            s_aneg, s_alog, AF.Exp), sem=ssem, waits=[(dsem, wconst)])
        prog.emit("vector", lambda h: h.tensor_scalar(
            s_aneg, s_aneg, -1.0, None, op0=OP.mult),
            sem=vsem, waits=[(ssem, prog.n(ssem))])
        prog.emit("vector", lambda h: h.tensor_copy(
            s_xpj_all, s_xpjf), sem=vsem, waits=[(dsem, wconst)])

        NT = 32
        evac1_done = {}
        ln_done_v = [0] * NT
        sq_done_s = [0] * NT
        tp_prev = 0
        for tt in range(NT):
            buf = s_x[tt % 2]
            xsem = xs[tt % 2]
            w = []
            if tt >= 2:
                w = [(vsem, ln_done_v[tt - 2]), (ssem, sq_done_s[tt - 2])]
            prog.dma("sync", buf[:], x_in[128 * tt:128 * (tt + 1), :], xsem, waits=w)
            wd = prog.n(xsem)
            exw = [(xsem, wd)]
            if tt >= 1:
                exw.append((tsem, tp_prev))
            lv, ls = emit_plain_ln(prog, sems, lnb1, buf[:], s_xln, D,
                                   extra_in_waits=exw)
            ln_done_v[tt] = lv
            sq_done_s[tt] = ls
            for dc8 in range(8):
                pb = pp[dc8 % 2]
                g1 = tt * 8 + dc8
                tw = [(vsem, lv), (dsem, wconst)]
                if g1 >= 2:
                    tw.append((vsem, evac1_done[g1 - 2]))
                prog.emit("tensor", lambda h, pb=pb, dc8=dc8: h.transpose(
                    pb[:, 0:128], sp_t1[:, 128 * dc8:128 * (dc8 + 1)], s_idf[:]),
                    sem=tsem, waits=tw)
                prog.emit("vector", lambda h, pb=pb, dc8=dc8, tt=tt: h.tensor_copy(
                    x_bf[dc8][:, 128 * tt:128 * (tt + 1)], pb[:, 0:128]),
                    sem=vsem, waits=[(tsem, prog.n(tsem))])
                evac1_done[g1] = prog.n(vsem)
            tp_prev = prog.n(tsem)
        wxbf = prog.n(vsem)

        for nchunk in range(8):
            wv = prog.n(vsem)
            for kc in range(8):
                prog.emit("tensor", lambda h, nchunk=nchunk, kc=kc: h.matmul(
                    pp[2][0:96, :], s_xpj_all[:, 96 * kc:96 * (kc + 1)],
                    x_bf[kc][:, 512 * nchunk:512 * (nchunk + 1)],
                    start=(kc == 0), stop=(kc == 7)),
                    sem=tsem, waits=[(vsem, max(wxbf, wv))])
            wt = prog.n(tsem)
            prog.emit("vector", lambda h, nchunk=nchunk: h.tensor_copy(
                deltar_bf[0:64, 512 * nchunk:512 * (nchunk + 1)], pp[2][0:64, :]),
                sem=vsem, waits=[(tsem, wt)])
            prog.emit("vector", lambda h, nchunk=nchunk: h.tensor_copy(
                bcpad_bf[64:96, 512 * nchunk:512 * (nchunk + 1)], pp[2][64:96, :]),
                sem=vsem)
        wbc = prog.n(vsem)
        prog.dma("sync", bc_dram, bcpad_bf[64:96, :], bcsem, waits=[(vsem, wbc)])
        wbcd = prog.n(bcsem)

        _p1_state = {}
        hc_done = {}
        bx_done = {}
        scan_done = {}
        ymm_done = {}
        scan_prev = [0]
        devac_done = {}
        for dc in range(NDC):
            w_pre = prog.n(vsem)
            ws_pre = prog.n(ssem)
            for nchunk in range(8):
                bank = pp[2 + (nchunk % 2)]
                tw = [(vsem, max(w_pre, devac_done.get((dc, nchunk - 2), 0))),
                      (ssem, ws_pre)]
                prog.emit("tensor", lambda h, dc=dc, nchunk=nchunk, bank=bank:
                          h.matmul(
                              bank[:], s_dtw_bf[:, 128 * dc:128 * (dc + 1)],
                              deltar_bf[0:64, 512 * nchunk:512 * (nchunk + 1)],
                              start=True, stop=True),
                          sem=tsem, waits=tw)
                prog.emit("vector", lambda h, nchunk=nchunk, dc=dc, bank=bank:
                          h.tensor_scalar(
                              delta_dc[:, 512 * nchunk:512 * (nchunk + 1)],
                              bank[:], s_dtb[:, dc:dc + 1], None, op0=OP.add),
                          sem=vsem, waits=[(tsem, prog.n(tsem))])
                devac_done[(dc, nchunk)] = prog.n(vsem)
            wdl = prog.n(vsem)
            prog.emit("scalar", lambda h: h.activation(
                sp_t1, delta_dc, AF.Abs), sem=ssem, waits=[(vsem, wdl)])
            prog.emit("scalar", lambda h: h.activation(
                sp_t1, sp_t1, AF.Exp, scale=-1.0), sem=ssem)
            prog.emit("scalar", lambda h: h.activation(
                sp_t1, sp_t1, AF.Ln, bias=1.0), sem=ssem)
            wsp = prog.n(ssem)
            prog.emit("vector", lambda h: h.tensor_scalar(
                delta_dc, delta_dc, 0.0, None, op0=OP.max),
                sem=vsem, waits=[(ssem, wsp)])
            prog.emit("vector", lambda h: h.tensor_tensor(
                delta_dc, delta_dc, sp_t1, op=OP.add), sem=vsem)
            prog.emit("vector", lambda h, dc=dc: h.tensor_tensor(
                dx_dc, delta_dc, x_bf[dc][:, 0:T], op=OP.mult), sem=vsem)
            wdx = prog.n(vsem)

            dA_b = [dA0, sp_t1]
            bxt_b = [bxt0, x_bf[4]]
            hc_b = [hc0, x_bf[5]]
            brep_b = [b_rep0, x_bf[6]]
            crep_b = [c_rep0, x_bf[7]]
            if dc not in _p1_state:
                _p1_state[dc] = True
            for n in range(N):
                par = n % 2
                bsem = bsem2[par]
                b_rep, c_rep = brep_b[par], crep_b[par]
                dA, bxt, hc_t = dA_b[par], bxt_b[par], hc_b[par]
                prog.dma("sync", b_rep,
                         bc_dram[n:n + 1, :].broadcast_to([128, T]), bsem,
                         waits=[(bcsem, wbcd),
                                (vsem, hc_done.get((dc, n - 2), scan_prev[0]))])
                prog.dma("scalar", c_rep,
                         bc_dram[16 + n:17 + n, :].broadcast_to([128, T]), bsem,
                         waits=[(bcsem, wbcd),
                                (vsem, hc_done.get((dc, n - 2), scan_prev[0]))])
                wb = prog.n(bsem)
                prog.emit("scalar", lambda h, dc=dc, n=n, dA=dA: h.activation(
                    dA, delta_dc, AF.Exp,
                    scale=s_aneg[:, dc * N + n:dc * N + n + 1]),
                    sem=ssem,
                    waits=[(vsem, max(wdx,
                                      scan_done.get((dc, n - 2), scan_prev[0])))])
                wda = prog.n(ssem)
                prog.emit("vector", lambda h, bxt=bxt, b_rep=b_rep:
                          h.tensor_tensor(bxt, dx_dc, b_rep, op=OP.mult),
                          sem=vsem, waits=[(bsem, wb), (vsem, wdx)])
                bx_done[(dc, n)] = prog.n(vsem)
                prog.emit("vector", lambda h, dA=dA, bxt=bxt:
                          h.tensor_tensor_scan(
                              h_t, dA, bxt, 0.0, op0=OP.mult, op1=OP.add),
                          sem=vsem, waits=[(ssem, wda)])
                scan_done[(dc, n)] = prog.n(vsem)
                prog.emit("vector", lambda h, hc_t=hc_t, c_rep=c_rep:
                          h.tensor_tensor(hc_t, h_t, c_rep, op=OP.mult),
                          sem=vsem,
                          waits=[(tsem, ymm_done.get((dc, n - 2), 0))])
                whc = prog.n(vsem)
                hc_done[(dc, n)] = whc
                for tc in range(8):
                    prog.emit("tensor", lambda h, tc=tc, n=n, hc_t=hc_t:
                              h.matmul(
                                  pp[tc][:], s_idb[:],
                                  hc_t[:, 512 * tc:512 * (tc + 1)],
                                  start=(n == 0), stop=(n == N - 1)),
                              sem=tsem, waits=[(vsem, whc)])
                ymm_done[(dc, n)] = prog.n(tsem)
            scan_prev[0] = prog.n(vsem)
            wyt = prog.n(tsem)
            for tc in range(8):
                prog.emit("vector", lambda h, dc=dc, tc=tc: h.scalar_tensor_tensor(
                    y_stage,
                    x_bf[dc][:, 512 * tc:512 * (tc + 1)],
                    s_dpar[:, dc:dc + 1], pp[tc][:],
                    op0=OP.mult, op1=OP.add),
                    sem=vsem, waits=[(tsem, wyt)])
                wys = prog.n(vsem)
                par = tc % 2
                ob = ytm[par]
                for sub in range(4):
                    tw = [(vsem, wys)] if sub == 0 else [(vsem, prog.n(vsem))]
                    prog.emit("tensor", lambda h, tc=tc, sub=sub: h.transpose(
                        pp[tc][:, 0:128],
                        y_stage[:, 128 * sub:128 * (sub + 1)], s_idf[:]),
                        sem=tsem, waits=tw)
                    wtp = prog.n(tsem)
                    ow = [(tsem, wtp)]
                    if sub == 0:
                        ow.append((osm[par], prog.n(osm[par])))
                    prog.emit("vector", lambda h, sub=sub, ob=ob, tc=tc:
                              h.tensor_copy(ob[:, 128 * sub:128 * (sub + 1)],
                                            pp[tc][:, 0:128]),
                              sem=vsem, waits=ow)
                prog.dma("sync",
                         y_loc[512 * tc:512 * (tc + 1), 128 * dc:128 * (dc + 1)]
                         .rearrange("(s p) d -> p s d", s=4),
                         ob.rearrange("p (s d) -> p s d", s=4),
                         osm[par], waits=[(vsem, prog.n(vsem))])

        c1 = sem("c1")
        prog.emit("gpsimd", lambda h: h.collective_compute(
            "AllGather", OP.bypass,
            replica_groups=[[0, 1], [2, 3], [4, 5], [6, 7]],
            ins=[y_loc.opt()], outs=[y_cat.opt()]),
            sem=c1, inc=1,
            waits=[(osm[0], prog.n(osm[0])), (osm[1], prog.n(osm[1]))])
        barrier(prog, skip=("c1",))
        ar.reset()

        dsem = sem("p2d"); vsem = sem("p2v"); ssem = sem("p2s"); tsem = sem("p2t")
        ys2 = [sem("p2y0"), sem("p2y1")]
        xs2p = [sem("p2x0"), sem("p2x1")]
        wsem = sem("p2w"); osem = sem("p2o")
        ixsem = sem("p2ix")
        sems = dict(vsem=vsem, ssem=ssem)
        prog.bind("vector", vsem); prog.bind("scalar", ssem)

        s_y2 = [ar.al(D, F32) for _ in range(2)]
        lnbc = ar.al(2 * D, F32)
        x1_sb = ar.al(16 * D, F32)
        x1fm = [ar.al(2048, BF16) for _ in range(8)]
        wstg2 = ar.al(D, F32)
        wstg2b = ar.al(D, F32)
        cosb = [ar.al(D, BF16) for _ in range(8)]
        sinb = [ar.al(D, BF16) for _ in range(8)]
        r_sq = ar.al(512, F32)
        i_sq = ar.al(512, F32)
        r_sq2 = ar.al(512, F32)
        i_sq2 = ar.al(512, F32)
        ffn_t = ar.al(D, F32)
        s_idx = ar.al(32, I32)

        prog.dma("sync", lnbc[:, 0:D], lnab[0:1, :].broadcast_to([128, D]), dsem)
        prog.dma("sync", lnbc[:, D:2 * D], lnab[1:2, :].broadcast_to([128, D]), dsem)
        prog.dma("sync", s_idx, idx_y, ixsem)
        wc2 = prog.n(dsem)
        wix = prog.n(ixsem)

        for c in range(8):
            prog.dma("sync", wstg2, theta[128 * c:128 * (c + 1), :], wsem,
                     waits=[(vsem, prog.n(vsem)), (ssem, prog.n(ssem))])
            prog.dma("sync", wstg2b, thn[128 * c:128 * (c + 1), :], wsem)
            wwd = prog.n(wsem)
            prog.emit("vector", lambda h: h.tensor_tensor(
                wstg2, wstg2, wstg2b, op=OP.add),
                sem=vsem, waits=[(wsem, wwd)])
            prog.emit("scalar", lambda h, c=c: h.activation(
                cosb[c], wstg2, AF.Sin, bias=st["pih"][:]),
                sem=ssem, waits=[(vsem, prog.n(vsem))])
            prog.dma("sync", wstg2, phi[128 * c:128 * (c + 1), :], wsem,
                     waits=[(vsem, prog.n(vsem)), (ssem, prog.n(ssem))])
            prog.dma("sync", wstg2b, phn[128 * c:128 * (c + 1), :], wsem)
            wwd = prog.n(wsem)
            prog.emit("vector", lambda h: h.tensor_tensor(
                wstg2, wstg2, wstg2b, op=OP.add),
                sem=vsem, waits=[(wsem, wwd)])
            prog.emit("scalar", lambda h, c=c: h.activation(
                sinb[c], wstg2, AF.Sin), sem=ssem,
                waits=[(vsem, prog.n(vsem))])
        wtrig = prog.n(ssem)

        NT2 = 16
        s_tB = ar.al(D, F32)
        s_zB = ar.al(D, F32)
        stB = {k: ar.al(1, F32) for k in ("ssum", "ssq", "sm", "sv", "sr")}
        lnbB = dict(sq=s_tB, ssum=stB["ssum"], ssq=stB["ssq"],
                    sm=stB["sm"], sv=stB["sv"], sr=stB["sr"],
                    eps=st["eps"][:])
        lnbA = dict(lnb)
        tp2 = {}
        yread_done = {}
        bdone = {}
        wdx = {}

        def y_stage_a(tt):
            p = tt % 2
            buf = s_y2[p]
            ysem = ys2[p]
            stp = st if p == 0 else None
            sq_t = s_t[:, 0:D] if p == 0 else s_tB
            ssum = st["ssum"][:] if p == 0 else stB["ssum"]
            ssq = st["ssq"][:] if p == 0 else stB["ssq"]
            sm = st["sm"][:] if p == 0 else stB["sm"]
            sv = st["sv"][:] if p == 0 else stB["sv"]
            sr = st["sr"][:] if p == 0 else stB["sr"]
            w = [(c1, 1), (ixsem, wix)]
            if tt >= 2:
                w.append((vsem, yread_done[tt - 2]))
            prog.emit("gpsimd", lambda h, buf=buf, tt=tt: h.indirect_dma_start(
                buf[:, 0:512], None, y_cat,
                IndirectOffsetOnAxis(ap=s_idx[:, 2 * tt:2 * tt + 1], axis=0)),
                sem=ysem, inc=16, waits=w)
            prog.emit("gpsimd", lambda h, buf=buf, tt=tt: h.indirect_dma_start(
                buf[:, 512:1024], None, y_cat,
                IndirectOffsetOnAxis(ap=s_idx[:, 2 * tt + 1:2 * tt + 2], axis=0)),
                sem=ysem, inc=16)
            prog.dma("sync", s_x[p][:], x_res[128 * tt:128 * (tt + 1), :],
                     xs2p[p], waits=[(vsem, prog.n(vsem))] if tt >= 1 else [])
            wd = (ysem, prog.n(ysem))
            wdx[tt] = (xs2p[p], prog.n(xs2p[p]))
            prog.emit("scalar", lambda h, buf=buf, sq_t=sq_t: h.activation(
                sq_t, buf[:, 0:D], AF.Square),
                sem=ssem, waits=[wd, (vsem, bdone.get(tt - 2, 0))])
            ws = prog.n(ssem)
            prog.emit("vector", lambda h, buf=buf, ssum=ssum: h.tensor_reduce(
                ssum, buf[:, 0:D].rearrange("p (a b) -> p a b", b=32),
                axis=AX.XY, op=OP.add), sem=vsem,
                waits=[wd, (vsem, bdone.get(tt - 2, 0))])
            prog.emit("vector", lambda h, sq_t=sq_t, ssq=ssq: h.tensor_reduce(
                ssq, sq_t.rearrange("p (a b) -> p a b", b=32),
                axis=AX.XY, op=OP.add), sem=vsem, waits=[(ssem, ws)])
            prog.emit("vector", lambda h, sm=sm, ssum=ssum: h.tensor_scalar(
                sm, ssum, -1.0 / D, None, op0=OP.mult),
                sem=vsem)
            prog.emit("vector", lambda h, sv=sv, sm=sm: h.tensor_tensor(
                sv, sm, sm, op=OP.mult), sem=vsem)
            prog.emit("vector", lambda h, sv=sv, ssq=ssq: h.scalar_tensor_tensor(
                sv, ssq, 1.0 / D, sv,
                op0=OP.mult, op1=OP.subtract), sem=vsem)
            prog.emit("scalar", lambda h, sr=sr, sv=sv: h.activation(
                sr, sv, AF.Sqrt, scale=D / (D - 1.0)),
                sem=ssem, waits=[(vsem, prog.n(vsem))])
            prog.emit("vector", lambda h, sr=sr: h.tensor_tensor(
                sr, sr, st["e12"][:], op=OP.add),
                sem=vsem, waits=[(ssem, prog.n(ssem))])
            prog.emit("vector", lambda h, sr=sr: h.reciprocal(sr, sr),
                      sem=vsem)

        def y_stage_b(tt):
            p = tt % 2
            buf = s_y2[p]
            sz = s_z[:] if p == 0 else s_zB
            sm = st["sm"][:] if p == 0 else stB["sm"]
            sr = st["sr"][:] if p == 0 else stB["sr"]
            lnbp = lnbA if p == 0 else lnbB
            prog.emit("vector", lambda h, buf=buf, sz=sz, sm=sm, sr=sr:
                      h.tensor_scalar(
                          sz, buf[:, 0:D], sm, sr,
                          op0=OP.add, op1=OP.mult), sem=vsem,
                      waits=[(tsem, tp2.get(tt - 2, 0))])
            yread_done[tt] = prog.n(vsem)
            prog.emit("vector", lambda h, sz=sz: h.tensor_tensor(
                sz, sz, lnbc[:, 0:D], op=OP.mult), sem=vsem,
                waits=[(dsem, wc2)])
            prog.emit("vector", lambda h, sz=sz: h.tensor_tensor(
                sz, sz, lnbc[:, D:2 * D], op=OP.add), sem=vsem)
            emit_plain_ln(prog, sems, lnbp, sz, sz, D,
                          extra_in_waits=[(vsem, prog.n(vsem))])
            prog.emit("vector", lambda h, tt=tt, sz=sz: h.tensor_tensor(
                x1_sb[:, D * tt:D * (tt + 1)], s_x[p][:], sz, op=OP.add),
                sem=vsem, waits=[wdx[tt]])
            emit_plain_ln(prog, sems, lnbp,
                          x1_sb[:, D * tt:D * (tt + 1)], sz, D,
                          extra_in_waits=[(vsem, prog.n(vsem))])
            wn = prog.n(vsem)
            for dc8 in range(8):
                pb = pp[dc8 % 2]
                tw = [(vsem, wn)]
                if tt * 8 + dc8 >= 1:
                    tw.append((vsem, prog.n(vsem)))
                prog.emit("tensor", lambda h, pb=pb, dc8=dc8, sz=sz: h.transpose(
                    pb[:, 0:128], sz[:, 128 * dc8:128 * (dc8 + 1)], s_idf[:]),
                    sem=tsem, waits=tw)
                prog.emit("vector", lambda h, pb=pb, dc8=dc8, tt=tt: h.tensor_copy(
                    x1fm[dc8][:, 128 * tt:128 * (tt + 1)], pb[:, 0:128]),
                    sem=vsem, waits=[(tsem, prog.n(tsem))])
            tp2[tt] = prog.n(tsem)
            bdone[tt] = prog.n(vsem)

        y_stage_a(0)
        for tt in range(NT2):
            if tt + 1 < NT2:
                y_stage_a(tt + 1)
            y_stage_b(tt)
        wxfm = prog.n(vsem)


        fsq = {}
        for tt in range(NT2):
            for oc in range(2):
                idx = 2 * tt + oc
                bre, bim = pp[2 + 2 * (oc % 2)], pp[3 + 2 * (oc % 2)]
                rs_, is_ = (r_sq, i_sq) if idx % 2 == 0 else (r_sq2, i_sq2)
                for kc in range(8):
                    prog.emit("tensor", lambda h, kc=kc, tt=tt, oc=oc, bre=bre:
                              h.matmul(bre[:], x1fm[kc][:, 128 * tt:128 * (tt + 1)],
                                       cosb[kc][:, 512 * oc:512 * (oc + 1)],
                                       start=(kc == 0), stop=(kc == 7)),
                              sem=tsem,
                              waits=[(vsem, wxfm), (ssem, wtrig),
                                     (ssem, fsq.get(idx - 2, 0))])
                    prog.emit("tensor", lambda h, kc=kc, tt=tt, oc=oc, bim=bim:
                              h.matmul(bim[:], x1fm[kc][:, 128 * tt:128 * (tt + 1)],
                                       sinb[kc][:, 512 * oc:512 * (oc + 1)],
                                       start=(kc == 0), stop=(kc == 7)),
                              sem=tsem)
                wt = prog.n(tsem)
                prog.emit("scalar", lambda h, bre=bre, rs_=rs_: h.activation(
                    rs_, bre[:], AF.Square), sem=ssem,
                    waits=[(tsem, wt), (vsem, prog.n(vsem))])
                prog.emit("scalar", lambda h, bim=bim, is_=is_: h.activation(
                    is_, bim[:], AF.Square), sem=ssem)
                fsq[idx] = prog.n(ssem)
                prog.emit("vector", lambda h, rs_=rs_, is_=is_: h.tensor_tensor(
                    rs_, rs_, is_, op=OP.add),
                    sem=vsem, waits=[(ssem, prog.n(ssem))])
                prog.emit("scalar", lambda h, oc=oc, rs_=rs_: h.activation(
                    ffn_t[:, 512 * oc:512 * (oc + 1)], rs_, AF.Sqrt),
                    sem=ssem, waits=[(vsem, prog.n(vsem)), (osem, prog.n(osem))])
            wsq = prog.n(ssem)
            emit_plain_ln(prog, sems, lnb, ffn_t, ffn_t, D,
                          extra_in_waits=[(ssem, wsq)])
            prog.emit("vector", lambda h, tt=tt: h.tensor_tensor(
                ffn_t, ffn_t, x1_sb[:, D * tt:D * (tt + 1)], op=OP.add),
                sem=vsem, waits=[(osem, prog.n(osem))])
            prog.dma("sync", x2_loc[128 * tt:128 * (tt + 1), :], ffn_t, osem,
                     waits=[(vsem, prog.n(vsem))])

        c2 = sem("c2")
        prog.emit("gpsimd", lambda h: h.collective_compute(
            "AllGather", OP.bypass,
            replica_groups=[[0, 1], [2, 3], [4, 5], [6, 7]],
            ins=[x2_loc.opt()], outs=[x2cat.opt()]),
            sem=c2, inc=1, waits=[(osem, prog.n(osem))])
        barrier(prog, skip=("c2",))
        ar.reset()
        _phase3_4(prog, ar, pp, lnb, st, s_x, s_z, s_t, s_idf, s_idb, sem, sbt,
                  c2, x2cat, x2_loc, k_dram, xfm_dram, v_dram, rsraw_dram,
                  rsf_dram, nb_dram, x3_loc, out_d,
                  wk_l, wv_l, wq_l, wo_l, bk_in, bq_in, bv_in, bo_in, lt_in,
                  idx_vb, w4_in, b4_in)
        block = ctx.enter_context(nc.Block())
        prog.run(block)
    return nc


def _phase3_4(prog, ar, pp, lnb, st, s_x, s_z, s_t, s_idf, s_idb, sem,
              sbt,
              
              c2, x2cat, x2_loc, k_dram, xfm_dram, v_dram, rsraw_dram,
              rsf_dram, nb_dram, x3_loc, out_d,
              wk_l, wv_l, wq_l, wo_l, bk_in, bq_in, bv_in, bo_in, lt_in,
              idx_vb, w4_in, b4_in):
    D = 1024
    T_ = 4096
    dsem = sem("p3d"); vsem = sem("p3v"); ssem = sem("p3s"); tsem = sem("p3t")
    xs3 = [sem("p3x0"), sem("p3x1")]
    wsem = sem("p3w"); kw = sem("p3kw")
    kwk = [sem("p3kwa"), sem("p3kwb")]
    xqs = [sem("p3xqa"), sem("p3xqb")]
    kr = [sem("p3k0"), sem("p3k1")]
    xqw = sem("p3xq"); wqw = sem("p3wq"); wow = sem("p3wo")
    sh = sem("p3sh"); rsm = sem("p3rs"); vbw = sem("p3vb")
    osem3 = sem("p3o"); xrs = sem("p3xr"); ivx = sem("p3ivx")
    sems = dict(vsem=vsem, ssem=ssem)
    prog.bind("vector", vsem); prog.bind("scalar", ssem)

    xfm4 = ar.al(4096, BF16)
    wstg = ar.al(1792, F32)
    ovl = ar.al(16512, BF16)
    wkA = ovl[:, 0:8192]
    wvA = ovl[:, 8192:16512]
    vtm = ar.al(32 * 1040, BF16)
    kev = ar.al(512, BF16)
    kev2 = ar.al(512, BF16)
    s_bk = ar.al(8, F32)
    s_bq = ar.al(8, F32)
    bo_bc = ar.al(1024, BF16)
    slt_b = [ar.al(12 * 128, BF16), ar.al(12 * 128, BF16)]
    bv_bc = slt_b[0][:, 0:1040]
    ksb = [ovl[:, 0:4096], ovl[:, 4096:8192]]
    q_sb = ovl[:, 8192:12288]
    xq_sb = ovl[:, 12288:16384]
    wq_sb = ar.al(1024, BF16)
    wo_sb = ar.al(8192, BF16)
    expt = [ar.al(512, BF16) for _ in range(4)]
    att = xfm4
    g3 = ar.al(4096, BF16)
    v_band = ar.al(6 * 1040, BF16)
    rsreg = ar.al(8192, BF16)
    o_tm4 = rsreg.bitcast(F32)
    o_tm = s_z[0:128, 0:D]
    ff_t = sbt("p3_ffc", [128, 512], F32)
    ff_c = ff_t[:]
    rs16b_h = sbt("p3_rs16b", [16, 512], BF16)
    rs16_h = sbt("p3_rs16", [16, 512], F32)
    rs16b = rs16b_h[:]
    rs16 = rs16_h[:]
    small_h = sbt("p3_small", [128, 57], F32)
    kmax = small_h[0:2, 0:8]
    kred = small_h[0:2, 8:9]
    qmax = small_h[0:2, 9:17]
    nbias = small_h[0:2, 17:25]
    nb_bc = small_h[:, 25:41]
    nb_bc2 = small_h[:, 41:57]
    s_ivb_h = sbt("p3_ivb", [128, 24], I32)
    s_ivb = s_ivb_h[:]

    for dst, src_ in [(s_bk, bk_in), (s_bq, bq_in)]:
        prog.dma("sync", dst, src_, dsem)
    prog.dma("sync", wstg[:, 0:1040], bv_in.broadcast_to([128, 1040]), dsem)
    prog.dma("sync", s_ivb, idx_vb, dsem)
    wc1 = prog.n(dsem)
    prog.emit("vector", lambda h: h.tensor_copy(bv_bc, wstg[:, 0:1040]),
              sem=vsem, waits=[(dsem, wc1)])
    prog.dma("sync", wstg[:, 0:1024], bo_in.broadcast_to([128, 1024]), dsem,
             waits=[(vsem, prog.n(vsem))])
    prog.emit("vector", lambda h: h.tensor_copy(bo_bc, wstg[:, 0:1024]),
              sem=vsem, waits=[(dsem, prog.n(dsem))])
    for jc in range(2):
        for lo, hi in [(0, 1792), (1792, 3584), (3584, 4096)]:
            prog.dma("sync", wstg[:, 0:hi - lo], wo_l[jc][:, lo:hi], wsem,
                     waits=[(vsem, prog.n(vsem))])
            prog.emit("vector", lambda h, jc=jc, lo=lo, hi=hi: h.tensor_copy(
                wo_sb[:, 4096 * jc + lo:4096 * jc + hi], wstg[:, 0:hi - lo]),
                sem=vsem, waits=[(wsem, prog.n(wsem))])
    for part in range(5):
        lo, hi = 1792 * part, min(1792 * (part + 1), 8192)
        prog.dma("sync", wstg[:, 0:hi - lo], wk_l[:, lo:hi], wsem,
                 waits=[(vsem, prog.n(vsem))])
        prog.emit("vector", lambda h, lo=lo, hi=hi: h.tensor_copy(
            wkA[:, lo:hi], wstg[:, 0:hi - lo]),
            sem=vsem, waits=[(wsem, prog.n(wsem))])
    for part in range(5):
        lo, hi = 1792 * part, min(1792 * (part + 1), 8320)
        prog.dma("sync", wstg[:, 0:hi - lo], wv_l[:, lo:hi], wsem,
                 waits=[(vsem, prog.n(vsem))])
        prog.emit("vector", lambda h, lo=lo, hi=hi: h.tensor_copy(
            wvA[:, lo:hi], wstg[:, 0:hi - lo]),
            sem=vsem, waits=[(wsem, prog.n(wsem))])
    wk_ready = prog.n(vsem)
    wwo_res = prog.n(vsem)
    prog.emit("vector", lambda h: h.memset(kmax, 0.0), sem=vsem)

    xfm_b = [xfm4, g3]
    tpb = [pp[0], pp[1], pp[7]]
    kevac_done = {}
    kdma_done = {}
    ksq_done = {}
    knmm_done = {}
    knred_done = {}
    vevac_done = {}
    ln_v_done = {}
    ln_s_done = {}
    tcopy_done = {}
    tile_tp = {}
    xfm_ready = {}
    kv_read = {}
    xq_dma = {}
    tfi = [0]

    def emit_tile(tt, src, col_ti, xfm, whaz_dma, whaz_c):
        buf = s_x[tt % 2]
        xsem = xs3[tt % 2]
        w = list(whaz_dma)
        if tt >= 2:
            w += [(vsem, ln_v_done[tt - 2]), (ssem, ln_s_done[tt - 2])]
        prog.dma("sync", buf[:], src, xsem, waits=w)
        exw = [(xsem, prog.n(xsem)), (tsem, tile_tp.get(tt - 1, 0))]
        emit_plain_ln(prog, sems, lnb, buf[:], s_z[:], D, extra_in_waits=exw)
        ln_v_done[tt] = prog.n(vsem)
        ln_s_done[tt] = prog.n(ssem)
        wn = prog.n(vsem)
        for dc8 in range(8):
            fi = tfi[0]
            pb = tpb[fi % 3]
            tw = [(vsem, wn), (dsem, wc1)]
            if fi >= 3:
                tw.append((vsem, tcopy_done[fi - 3]))
            prog.emit("tensor", lambda h, pb=pb, dc8=dc8: h.transpose(
                pb[:, 0:128], s_z[:, 128 * dc8:128 * (dc8 + 1)], s_idf[:]),
                sem=tsem, waits=tw)
            cw = [(tsem, prog.n(tsem))]
            if dc8 == 0:
                cw += list(whaz_c)
            prog.emit("vector",
                      lambda h, pb=pb, dc8=dc8, col_ti=col_ti, xfm=xfm:
                      h.tensor_copy(
                          xfm[:, 512 * dc8 + 128 * col_ti:
                              512 * dc8 + 128 * (col_ti + 1)],
                          pb[:, 0:128]),
                      sem=vsem, waits=cw)
            tcopy_done[fi] = prog.n(vsem)
            tfi[0] += 1
        tile_tp[tt] = prog.n(tsem)

    def emit_k(g, dqc):
        xfm = xfm_b[g % 2]
        bank = pp[2 + (dqc % 2)]
        kv = kev if dqc % 2 == 0 else kev2
        wbase = max(xfm_ready[g], wk_ready)
        for c in range(8):
            tw = [(vsem, max(wbase, kevac_done.get(g * 8 + dqc - 2, 0)))]
            prog.emit("tensor", lambda h, dqc=dqc, c=c, bank=bank, xfm=xfm:
                      h.matmul(
                          bank[:],
                          wkA[:, 1024 * dqc + 128 * c:1024 * dqc + 128 * (c + 1)],
                          xfm[:, 512 * c:512 * (c + 1)],
                          start=(c == 0), stop=(c == 7)),
                      sem=tsem, waits=tw)
        wt = prog.n(tsem)
        prog.emit("vector", lambda h, dqc=dqc, bank=bank, kv=kv:
                  h.tensor_scalar(
                      kv, bank[:], s_bk[:, dqc:dqc + 1], None, op0=OP.add),
                  sem=vsem,
                  waits=[(tsem, wt),
                         kdma_done.get(g * 8 + dqc - 2, (kw, 0)),
                         (ssem, ksq_done.get(g * 8 + dqc - 2, 0))])
        wkev = prog.n(vsem)
        kevac_done[g * 8 + dqc] = wkev
        prog.dma("gpsimd", k_dram[dqc, :, 512 * g:512 * (g + 1)], kv,
                 kwk[dqc % 2], waits=[(vsem, wkev)])
        kdma_done[g * 8 + dqc] = (kwk[dqc % 2], prog.n(kwk[dqc % 2]))
        prog.emit("scalar", lambda h, kv=kv: h.activation(
            expt[0], kv, AF.Square), sem=ssem,
            waits=[(vsem, wkev), (tsem, knmm_done.get(g * 8 + dqc - 1, 0))])
        ksq_done[g * 8 + dqc] = prog.n(ssem)
        prog.emit("tensor", lambda h: h.matmul(
            pp[6][0:2, :], s_idb[:, 0:2], expt[0],
            start=True, stop=True), sem=tsem,
            waits=[(ssem, prog.n(ssem)),
                   (vsem, knred_done.get(g * 8 + dqc - 1, 0))])
        knmm_done[g * 8 + dqc] = prog.n(tsem)
        prog.emit("vector", lambda h: h.tensor_reduce(
            kred, pp[6][0:2, :], axis=AX.X, op=OP.max),
            sem=vsem, waits=[(tsem, prog.n(tsem))])
        prog.emit("vector", lambda h, dqc=dqc: h.tensor_tensor(
            kmax[:, dqc:dqc + 1], kmax[:, dqc:dqc + 1], kred,
            op=OP.max), sem=vsem)
        knred_done[g * 8 + dqc] = prog.n(vsem)

    def emit_v(g, ti, oc):
        xfm = xfm_b[g % 2]
        lo = 512 * oc
        hi = min(512 * (oc + 1), 1040)
        vidx = ti * 3 + oc
        bank = pp[4 + (vidx % 2)]
        wbase = max(xfm_ready[g], wk_ready)
        for c in range(8):
            tw = [(vsem, max(wbase, vevac_done.get(g * 12 + vidx - 2, 0)))]
            prog.emit("tensor", lambda h, ti=ti, c=c, lo=lo, hi=hi,
                      bank=bank, xfm=xfm: h.matmul(
                          bank[:, 0:hi - lo],
                          xfm[:, 512 * c + 128 * ti:512 * c + 128 * (ti + 1)],
                          wvA[:, 1040 * c + lo:1040 * c + hi],
                          start=(c == 0), stop=(c == 7)),
                      sem=tsem, waits=tw)
        prog.emit("vector", lambda h, g=g, ti=ti, lo=lo, hi=hi, bank=bank:
                  h.scalar_tensor_tensor(
                      vtm[:, 1040 * (4 * g + ti) + lo:1040 * (4 * g + ti) + hi],
                      bank[:, 0:hi - lo], 1.0,
                      bv_bc[:, lo:hi],
                      op0=OP.mult, op1=OP.add),
                  sem=vsem, waits=[(tsem, prog.n(tsem))])
        vevac_done[g * 12 + vidx] = prog.n(vsem)
        if oc == 2:
            wvt = prog.n(vsem)
            prog.dma("gpsimd",
                     v_dram[128 * (4 * g + ti):128 * (4 * g + ti + 1), :],
                     vtm[:, 1040 * (4 * g + ti):1040 * (4 * g + ti + 1)], kw,
                     waits=[(vsem, wvt)])

    for ti in range(4):
        emit_tile(ti, x2_loc[128 * ti:128 * (ti + 1), :], ti, xfm_b[0], [], [])
    xfm_ready_loc = {0: prog.n(vsem)}
    for lg in range(4):
        xfm = xfm_b[lg % 2]
        wgq = xfm_ready_loc[lg]
        for c in range(8):
            prog.dma("gpsimd", xfm_dram[c, :, 512 * lg:512 * (lg + 1)],
                     xfm[:, 512 * c:512 * (c + 1)], xqs[lg % 2],
                     waits=[(vsem, wgq)])
            if c in (1, 3, 5, 7):
                ti = c // 2
                if lg < 3:
                    ltt = 4 * (lg + 1) + ti
                    whaz_c = ([] if lg == 0 else
                              [(xqs[(lg - 1) % 2], xq_dma[lg - 1])])
                    emit_tile(ltt, x2_loc[128 * ltt:128 * (ltt + 1), :],
                              ti, xfm_b[(lg + 1) % 2], [], whaz_c)
                else:
                    emit_tile(16 + ti, x2cat[128 * ti:128 * (ti + 1), :],
                              ti, xfm_b[0],
                              [(c2, 1)] if ti < 2 else [],
                              [(xqs[0], xq_dma[2])])
        xq_dma[lg] = prog.n(xqs[lg % 2])
        if lg < 3:
            xfm_ready_loc[lg + 1] = prog.n(vsem)
    xfm_ready[0] = prog.n(vsem)

    for g in range(8):
        kv_units = [("k", dqc) for dqc in range(8)] + \
                   [("v", ti, oc) for ti in range(4) for oc in range(3)]
        if g < 7:
            nxt = [(16 + 4 * (g + 1) + ti,
                    x2cat[128 * (4 * (g + 1) + ti):
                          128 * (4 * (g + 1) + ti + 1), :],
                    ti, xfm_b[(g + 1) % 2],
                    [], [(tsem, kv_read[g - 1])] if g >= 1 else
                    [(xqs[1], xq_dma[3])])
                   for ti in range(4)]
        else:
            nxt = []
        ni = 0
        for i, u in enumerate(kv_units):
            if u[0] == "k":
                emit_k(g, u[1])
            else:
                emit_v(g, u[1], u[2])
            if i % 5 == 3 and ni < len(nxt):
                emit_tile(*nxt[ni])
                ni += 1
        while ni < len(nxt):
            emit_tile(*nxt[ni])
            ni += 1
        kv_read[g] = prog.n(tsem)
        xfm_ready[g + 1] = prog.n(vsem)
    wpassa_k = prog.n(kw)
    wpassa_ka = prog.n(kwk[0])
    wpassa_kb = prog.n(kwk[1])
    wpassa_xa = prog.n(xqs[0])
    wpassa_xb = prog.n(xqs[1])

    prog.emit("scalar", lambda h: h.activation(
        kmax, kmax, AF.Sqrt), sem=ssem, waits=[(vsem, prog.n(vsem))])
    wkmax = prog.n(ssem)

    nbsm = sem("p3nbs")
    rs2 = [sem("p3r0"), sem("p3r1")]
    lts = sem("p3lts")
    wqg = sem("p3wqg")
    nbb = [nb_bc, nb_bc2]
    avA, avB = pp[4], pp[5]
    qmm_done = {}
    qevac_all = {}
    wq_copy_last = {}
    lt_copy_last = {}
    wnb_q = {}
    exp_fin = {}
    av_fin = {}
    sc6 = {}
    scmm31 = {}
    shift_done = {}
    wvb_q = {}
    wrf_q = {}
    wlv_all = {}
    lv_mm_all = {}
    cmb_done = {}
    oproj_cd = {}
    oproj_all = {}
    oproj_ev = {}
    wev_q = {}
    rs_recip = {}
    ln_s_last = {}
    qst = {"redmm": 0}
    qpst = {}
    scd = {}
    qmax_done = {}

    def emit_qproj(q):
        xw = [(kw, wpassa_k), (kwk[0], wpassa_ka), (kwk[1], wpassa_kb),
              (xqs[0], wpassa_xa), (xqs[1], wpassa_xb),
              (tsem, prog.n(tsem))]
        kpw = [(kw, wpassa_k), (kwk[0], wpassa_ka), (kwk[1], wpassa_kb),
               (tsem, prog.n(tsem))]
        lcl = 0
        for lo, hi in [(0, 1024), (1024, 1536)]:
            prog.dma("scalar", wstg[:, 0:hi - lo],
                     lt_in[:, 1536 * q + lo:1536 * q + hi], lts,
                     waits=[(lts, prog.n(lts)), (vsem, lcl)])
            prog.emit("vector", lambda h, q=q, lo=lo, hi=hi: h.tensor_copy(
                slt_b[q % 2][:, lo:hi], wstg[:, 0:hi - lo]),
                sem=vsem, waits=[(lts, prog.n(lts))])
            lcl = prog.n(vsem)
        lt_copy_last[q] = lcl
        for c in range(8):
            prog.dma("scalar", xq_sb[:, 512 * c:512 * (c + 1)],
                     xfm_dram[c, :, 512 * q:512 * (q + 1)], xqw, waits=xw)
        wxq = prog.n(xqw)
        prog.dma("scalar", ksb[0], k_dram[0], kr[0], waits=kpw)
        qpst[q] = {"wqc": lcl, "wxq": wxq}
        for dqc in range(8):
            qp_unit(q, dqc)
        qp_post(q)

    def qp_pre(q):
        xw = [(tsem, qmm_done[q - 1])]
        for c in range(8):
            prog.dma("sync", xq_sb[:, 512 * c:512 * (c + 1)],
                     xfm_dram[c, :, 512 * q:512 * (q + 1)], xqw, waits=xw)
        qpst[q] = {"wqc": 0, "wxq": prog.n(xqw)}

    def qp_lt(q):
        lcl = wq_copy_last.get(q - 1, 0)
        for lo, hi in [(0, 1024), (1024, 1536)]:
            prog.dma("scalar", wstg[:, 0:hi - lo],
                     lt_in[:, 1536 * q + lo:1536 * q + hi], lts,
                     waits=[(lts, prog.n(lts)), (vsem, lcl),
                            (tsem, lv_mm_all.get(q - 2, 0))])
            prog.emit("vector", lambda h, q=q, lo=lo, hi=hi: h.tensor_copy(
                slt_b[q % 2][:, lo:hi], wstg[:, 0:hi - lo]),
                sem=vsem, waits=[(lts, prog.n(lts))])
            lcl = prog.n(vsem)
        lt_copy_last[q] = lcl
        qpst[q]["wqc"] = lcl

    def qp_ksb(q):
        prog.dma("scalar", ksb[0], k_dram[0], kr[0],
                 waits=[(tsem, scd[q - 1][6])])

    def qp_unit(q, dqc):
        st = qpst[q]
        wsm = wqg if q > 0 else wqw
        prog.dma("gpsimd" if q > 0 else "scalar", wstg[:, 0:1024], wq_l[dqc],
                 wsm, waits=[(wsm, prog.n(wsm)), (vsem, st["wqc"])])
        prog.emit("vector", lambda h: h.tensor_copy(wq_sb, wstg[:, 0:1024]),
                  sem=vsem, waits=[(wsm, prog.n(wsm)),
                                   (tsem, qmm_done.get(q - 1, 0))])
        st["wqc"] = prog.n(vsem)
        for c in range(8):
            prog.emit("tensor", lambda h, c=c: h.matmul(
                pp[6][:], wq_sb[:, 128 * c:128 * (c + 1)],
                xq_sb[:, 512 * c:512 * (c + 1)],
                start=(c == 0), stop=(c == 7)),
                sem=tsem, waits=[(xqw, st["wxq"]),
                                 (vsem, max(st["wqc"],
                                            oproj_ev.get(q - 2, 0)))])
        wt = prog.n(tsem)
        prog.emit("vector", lambda h, dqc=dqc: h.tensor_scalar(
            q_sb[:, 512 * dqc:512 * (dqc + 1)], pp[6][:],
            s_bq[:, dqc:dqc + 1], None, op0=OP.add),
            sem=vsem, waits=[(tsem, wt)])
        wev = prog.n(vsem)
        sqw = [(vsem, wev), (tsem, qst["redmm"])]
        if q == 0 and dqc == 0:
            sqw += [(kwk[0], wpassa_ka), (kwk[1], wpassa_kb)]
        prog.emit("scalar", lambda h, dqc=dqc: h.activation(
            kev, q_sb[:, 512 * dqc:512 * (dqc + 1)], AF.Square),
            sem=ssem, waits=sqw)
        prog.emit("tensor", lambda h: h.matmul(
            pp[7][0:2, :], s_idb[:, 0:2], kev,
            start=True, stop=True), sem=tsem,
            waits=[(ssem, prog.n(ssem)), (vsem, prog.n(vsem))])
        qst["redmm"] = prog.n(tsem)
        prog.emit("vector", lambda h: h.tensor_reduce(
            kred, pp[7][0:2, :], axis=AX.X, op=OP.max),
            sem=vsem, waits=[(tsem, qst["redmm"])])
        prog.emit("vector", lambda h, dqc=dqc: h.tensor_copy(
            qmax[:, dqc:dqc + 1], kred), sem=vsem)
        if dqc == 7:
            qmm_done[q] = prog.n(tsem)
            qevac_all[q] = wev
            wq_copy_last[q] = st["wqc"]
            qmax_done[q] = prog.n(vsem)

    def qp_post(q):
        prog.emit("scalar", lambda h: h.activation(
            qmax, qmax, AF.Sqrt), sem=ssem, waits=[(vsem, qmax_done[q])])
        prog.emit("vector", lambda h: h.tensor_tensor(
            nbias, qmax, kmax, op=OP.mult), sem=vsem,
            waits=[(ssem, prog.n(ssem)), (ssem, wkmax),
                   (nbsm, wnb_q.get(q - 1, 0))])
        prog.emit("vector", lambda h: h.tensor_scalar(
            nbias, nbias, -0.125, -0.0125, op0=OP.mult, op1=OP.add),
            sem=vsem)
        prog.dma("sync", nb_dram.rearrange("o (a b) -> (o a) b", b=8),
                 nbias, nbsm, waits=[(vsem, prog.n(vsem))])
        prog.dma("sync", nbb[q % 2], nb_dram.broadcast_to([128, 16]), nbsm,
                 waits=[(nbsm, prog.n(nbsm)), (ssem, exp_fin.get(q - 2, 0))])
        wnb_q[q] = prog.n(nbsm)

    def mk_tail(qt):
        def pre():
            prog.emit("gpsimd", lambda h: h.dma_start(att[64:128, :],
                                                      g3[0:64, :]),
                      sem=sh, inc=16, waits=[(vsem, wev_q[qt])])
            shift_done[qt] = prog.n(sh)
            for s6 in range(6):
                prog.emit("gpsimd", lambda h, s6=s6: h.indirect_dma_start(
                    v_band[:, 1040 * s6:1040 * (s6 + 1)], None, v_dram,
                    IndirectOffsetOnAxis(ap=s_ivb[:, 6 * qt + s6:
                                               6 * qt + s6 + 1], axis=0)),
                    sem=vbw, inc=16,
                    waits=[(kw, wpassa_k), (dsem, wc1),
                           (tsem, lv_mm_all.get(qt - 1, 0)),
                           (rs2[0], prog.n(rs2[0])),
                           (rs2[1], prog.n(rs2[1]))])
            wvb_q[qt] = prog.n(vbw)
            prog.dma("sync", rs16b,
                     rsraw_dram.rearrange("o (h j) -> (o h) j", j=512), rsm,
                     waits=[(rs2[0], prog.n(rs2[0])), (rs2[1], prog.n(rs2[1])),
                            (vsem, rs_recip.get(qt - 1, 0))])
            prog.emit("vector", lambda h: h.reciprocal(rs16, rs16b),
                      sem=vsem, waits=[(rsm, prog.n(rsm))])
            rs_recip[qt] = prog.n(vsem)
            prog.emit("vector", lambda h: h.tensor_scalar(
                rs16, rs16, 0.6, None, op0=OP.mult), sem=vsem)
            prog.dma("sync", rsf_dram.rearrange("o (h j) -> (o h) j", j=512),
                     rs16, rsm, waits=[(vsem, prog.n(vsem))])
            wrf_q[qt] = prog.n(rsm)

        def lv(u):
            qb, dvc = u % 4, u // 4
            for s3 in range(3):
                b_loc = qb + s3
                vview = (v_band[:, 1040 * b_loc:1040 * (b_loc + 1)]
                         .rearrange("p (h c) -> p h c", c=65)
                         [:, 8 * dvc:8 * (dvc + 1), 0:64])
                prog.emit("tensor", lambda h, qt=qt, qb=qb, s3=s3,
                          vview=vview: h.matmul(
                              pp[6][:],
                              slt_b[qt % 2][:, (3 * qb + s3) * 128:
                                            (3 * qb + s3 + 1) * 128],
                              vview, start=(s3 == 0), stop=(s3 == 2)),
                          sem=tsem, waits=[(vsem, prog.n(vsem)),
                                           (vbw, wvb_q[qt]),
                                           (lts, prog.n(lts))])
            wlv = prog.n(tsem)
            lv_mm_all[qt] = wlv
            for sub in range(4):
                prog.emit("vector", lambda h, sub=sub: h.tensor_copy(
                    s_t[:, 128 * sub:128 * (sub + 1)],
                    pp[6][:, 128 * sub:128 * (sub + 1)]),
                    sem=vsem, waits=[(tsem, wlv)])
            wlve = prog.n(vsem)
            for sub in range(4):
                prog.emit("tensor", lambda h, sub=sub: h.transpose(
                    pp[7][:, 0:128], s_t[:, 128 * sub:128 * (sub + 1)],
                    s_idf[:]), sem=tsem,
                    waits=[(vsem, wlve), (vsem, prog.n(vsem))])
                prog.emit("vector", lambda h, qb=qb, dvc=dvc, sub=sub:
                          h.tensor_copy(
                              g3[:, 512 * (4 * dvc + sub) + 128 * qb:
                                 512 * (4 * dvc + sub) + 128 * (qb + 1)],
                              pp[7][:, 0:128]),
                          sem=vsem, waits=[(tsem, prog.n(tsem)),
                                           (sh, shift_done[qt])])
            wlv_all[qt] = prog.n(vsem)

        def cmb(c):
            ffr = rsf_dram.rearrange("o (h j) -> (o h) j", j=512)
            prog.dma("sync", ff_c[0:64, :],
                     ffr[2 * c:2 * c + 1, :].broadcast_to([64, 512]), rsm,
                     waits=[(rsm, wrf_q[qt]), (vsem, prog.n(vsem))])
            prog.dma("sync", ff_c[64:128, :],
                     ffr[2 * c + 1:2 * c + 2, :].broadcast_to([64, 512]), rsm)
            prog.emit("vector", lambda h, c=c: h.tensor_tensor(
                att[:, 512 * c:512 * (c + 1)], att[:, 512 * c:512 * (c + 1)],
                ff_c, op=OP.mult),
                sem=vsem, waits=[(rsm, prog.n(rsm)), (sh, shift_done[qt])])
            prog.emit("vector", lambda h, c=c: h.tensor_tensor(
                att[:, 512 * c:512 * (c + 1)], att[:, 512 * c:512 * (c + 1)],
                g3[:, 512 * c:512 * (c + 1)], op=OP.add),
                sem=vsem, waits=[(vsem, wlv_all[qt])])
            cmb_done[qt] = prog.n(vsem)

        def opj():
            ocd = {}
            for jc in range(2):
                for qsub in range(4):
                    last = (jc == 1 and qsub == 3)
                    for c in range(8):
                        prog.emit("tensor", lambda h, jc=jc, qsub=qsub, c=c:
                                  h.matmul(
                                      pp[6][:],
                                      att[:, 512 * c + 128 * qsub:
                                          512 * c + 128 * (qsub + 1)],
                                      wo_sb[:, 4096 * jc + 512 * c:
                                            4096 * jc + 512 * (c + 1)],
                                      start=(c == 0), stop=(c == 7)),
                                  sem=tsem,
                                  waits=[(vsem, max(cmb_done[qt], wwo_res)),
                                         (vsem, prog.n(vsem))])
                        if last:
                            ocd[c] = prog.n(tsem)
                    ew = [(tsem, prog.n(tsem))]
                    if jc == 0 and qsub == 0:
                        ew.append((ssem, ln_s_last.get(qt - 1, 0)))
                    prog.emit("vector", lambda h, qsub=qsub, jc=jc:
                              h.scalar_tensor_tensor(
                                  o_tm4[:, 1024 * qsub + 512 * jc:
                                        1024 * qsub + 512 * (jc + 1)],
                                  pp[6][:], 1.0,
                                  bo_bc[:, 512 * jc:512 * (jc + 1)],
                                  op0=OP.mult, op1=OP.add),
                              sem=vsem, waits=ew)
            oproj_cd[qt] = ocd
            oproj_all[qt] = prog.n(tsem)
            oproj_ev[qt] = prog.n(vsem)

        def lnr(qsub):
            tt0 = 4 * qt + qsub
            emit_plain_ln(prog, sems, lnb,
                          o_tm4[:, 1024 * qsub:1024 * (qsub + 1)], o_tm, D,
                          extra_in_waits=[(vsem, prog.n(vsem)),
                                          (osem3, prog.n(osem3))])
            ln_s_last[qt] = prog.n(ssem)
            prog.dma("sync", s_x[0][:], x2_loc[128 * tt0:128 * (tt0 + 1), :],
                     xrs, waits=[(vsem, prog.n(vsem))])
            prog.emit("vector", lambda h: h.tensor_tensor(
                o_tm, o_tm, s_x[0][:], op=OP.add),
                sem=vsem, waits=[(xrs, prog.n(xrs)), (osem3, prog.n(osem3))])
            prog.dma("sync", x3_loc[128 * tt0:128 * (tt0 + 1), :], o_tm,
                     osem3, waits=[(vsem, prog.n(vsem))])
        return pre, lv, cmb, opj, lnr

    def emit_cloop(qt, units):
        for u in units.get("pre", []):
            u()
        exp_done = {}
        av_done = {}
        sc_done = {}
        scd[qt] = sc_done
        sc_mm_done = {}
        ev_done = {}
        efp = exp_fin.get(qt - 1, 0)
        afp = av_fin[qt - 1]
        wnb = wnb_q[qt]
        nbc = nbb[qt % 2]
        for c in range(8):
            ks = ksb[c % 2]
            ksem = kr[c % 2]
            wks = prog.n(kr[c % 2])
            if c + 1 < 8:
                prog.dma("scalar", ksb[(c + 1) % 2], k_dram[c + 1],
                         kr[(c + 1) % 2],
                         waits=[(tsem, sc_done.get(c - 1, prog.n(tsem)))])

            def emit_scores(kb):
                par = kb % 2
                sA = pp[par * 2]
                sB = pp[par * 2 + 1]
                mw = [(ksem, wks),
                      (ssem, exp_done.get((c, kb - 2),
                                          exp_done.get((c - 1, 31), efp)))]
                if c == 0 and kb < 2:
                    mw.append((vsem, qevac_all[qt]))
                prog.emit("tensor", lambda h, sA=sA, ks=ks, kb=kb, c=c:
                          h.matmul(sA[:], ks[0:64, 128 * kb:128 * (kb + 1)],
                                   q_sb[0:64, 512 * c:512 * (c + 1)],
                                   start=True, stop=True),
                          sem=tsem, waits=mw)
                prog.emit("tensor", lambda h, sB=sB, ks=ks, kb=kb, c=c:
                          h.matmul(sB[:], ks[64:128, 128 * kb:128 * (kb + 1)],
                                   q_sb[64:128, 512 * c:512 * (c + 1)],
                                   start=True, stop=True,
                                   tile_position=(64, 0)),
                          sem=tsem)
                sc_mm_done[(c, kb)] = prog.n(tsem)

            emit_scores(0)
            for kb in range(32):
                par = kb % 2
                sA = pp[par * 2]
                sB = pp[par * 2 + 1]
                eA = expt[par * 2]
                eB = expt[par * 2 + 1]
                ew = [(tsem, sc_mm_done[(c, kb)]), (nbsm, wnb),
                      (tsem, av_done.get((c, kb - 2), afp))]
                prog.emit("scalar", lambda h, sA=sA, c=c, eA=eA, nbc=nbc:
                          h.activation(
                              eA, sA[:], AF.Exp, scale=0.125,
                              bias=nbc[:, c:c + 1]),
                          sem=ssem, waits=ew)
                prog.emit("scalar", lambda h, sB=sB, c=c, eB=eB, nbc=nbc:
                          h.activation(
                              eB, sB[:], AF.Exp, scale=0.125,
                              bias=nbc[:, 8 + c:8 + c + 1]),
                          sem=ssem)
                we = prog.n(ssem)
                exp_done[(c, kb)] = we
                if kb + 1 < 32:
                    emit_scores(kb + 1)
                avw = [(ssem, we)]
                if kb == 0:
                    avw.append((vsem, ev_done.get(c - 1,
                                                  wev_q.get(qt - 1, 0))))
                prog.emit("tensor", lambda h, kb=kb, c=c, eA=eA:
                          h.matmul(
                              avA[0:65, :],
                              vtm[:, 1040 * kb + 130 * c:
                                  1040 * kb + 130 * c + 65],
                              eA, start=(kb == 0), stop=(kb == 31)),
                          sem=tsem, waits=avw)
                prog.emit("tensor", lambda h, kb=kb, c=c, eB=eB:
                          h.matmul(
                              avB[0:65, :],
                              vtm[:, 1040 * kb + 130 * c + 65:
                                  1040 * kb + 130 * c + 130],
                              eB, start=(kb == 0), stop=(kb == 31)),
                          sem=tsem)
                av_done[(c, kb)] = prog.n(tsem)
                for u in units.get((c, kb), []):
                    u()
            wav = prog.n(tsem)
            sc_done[c] = wav
            for u in units.get((c, 32), []):
                u()
            rw = [(tsem, wav), (rs2[0], prog.n(rs2[0])),
                  (rs2[1], prog.n(rs2[1])),
                  (tsem, lv_mm_all.get(qt - 1, 0)),
                  (vbw, wvb_q.get(qt - 1, 0))]
            prog.emit("vector", lambda h: h.tensor_copy(
                v_band[64:65, 0:512], avA[64:65, :]),
                sem=vsem, waits=rw)
            prog.emit("vector", lambda h: h.tensor_copy(
                v_band[64:65, 512:1024], avB[64:65, :]),
                sem=vsem)
            wrc = prog.n(vsem)
            prog.dma("sync", rsraw_dram[0:1, 1024 * c:1024 * c + 512],
                     v_band[64:65, 0:512], rs2[0], waits=[(vsem, wrc)])
            prog.dma("sync", rsraw_dram[0:1, 1024 * c + 512:1024 * (c + 1)],
                     v_band[64:65, 512:1024], rs2[1], waits=[(vsem, wrc)])
            evw = [(tsem, wav), (sh, shift_done.get(qt - 1, 0)),
                   (tsem, oproj_cd.get(qt - 1, {}).get(c, 0))]
            if qt == 0 and c == 0:
                evw += [(xqs[0], wpassa_xa), (xqs[1], wpassa_xb)]
            prog.emit("vector", lambda h, c=c: h.tensor_copy(
                att[0:64, 512 * c:512 * (c + 1)], avA[0:64, :]),
                sem=vsem, waits=evw)
            prog.emit("vector", lambda h, c=c: h.tensor_copy(
                g3[0:64, 512 * c:512 * (c + 1)], avB[0:64, :]),
                sem=vsem)
            ev_done[c] = prog.n(vsem)
        wev_q[qt] = prog.n(vsem)
        exp_fin[qt] = prog.n(ssem)
        av_fin[qt] = sc_done[7]
        sc6[qt] = sc_done[6]
        scmm31[qt] = sc_mm_done[(7, 31)]

    av_fin[-1] = prog.n(tsem)
    emit_qproj(0)
    for qt in range(4):
        units = {}

        def add(key, fn):
            units.setdefault(key, []).append(fn)

        if qt >= 1:
            pre, lv, cmb, opj, lnr = mk_tail(qt - 1)
            add("pre", pre)
            for u8 in range(8):
                add((0, 1 + 2 * u8), lambda u8=u8, lv=lv: lv(u8))
            for c8 in range(8):
                add((0, 17 + c8), lambda c8=c8, cmb=cmb: cmb(c8))
            add((0, 32), opj)
            for qs in range(4):
                add((1, 4 + 5 * qs), lambda qs=qs, lnr=lnr: lnr(qs))
        if qt < 3:
            q = qt + 1
            add("pre", lambda q=q: qp_pre(q))
            add((0, 27), lambda q=q: qp_lt(q))
            for dqc in range(7):
                add((dqc + 1, 26), lambda q=q, dqc=dqc: qp_unit(q, dqc))
            add((7, 10), lambda q=q: qp_ksb(q))
            add((7, 30), lambda q=q: qp_unit(q, 7))
        emit_cloop(qt, units)
        if qt < 3:
            qp_post(qt + 1)
    pre, lv, cmb, opj, lnr = mk_tail(3)
    pre()
    for u8 in range(8):
        lv(u8)
    for c8 in range(8):
        cmb(c8)
    opj()
    for qs in range(4):
        lnr(qs)
    barrier(prog)
    ar.reset()

    dsem = sem("p4d"); vsem = sem("p4v"); ssem = sem("p4s"); tsem = sem("p4t")
    wsem4 = sem("p4w")
    xs4 = [sem("p4x0"), sem("p4x1")]
    os4 = [sem("p4o0"), sem("p4o1")]
    sems = dict(vsem=vsem, ssem=ssem)
    prog.bind("vector", vsem); prog.bind("scalar", ssem)

    wstg4 = ar.al(D, F32)

    def mk_qb():
        d = dict(zt=ar.al(D, F32), i32t=ar.al(D, I32),
                 i_f=ar.al(D, F32), qs=ar.al(D, F32),
                 qb=ar.al(D, F32), az=ar.al(D, F32),
                 msk=ar.al(D, U32), q=ar.al(D, F32),
                 f8t=ar.al(D // 2, BF16).bitcast(mybir.dt.float8e4))
        d["wmax"] = ar.al(32, F32)
        d["scl"] = ar.al(32, F32)
        d["rcp"] = ar.al(32, F32)
        return d
    qb2 = [mk_qb(), mk_qb()]
    qb_ = qb2[0]
    wd_c = ar.al(D, BF16)
    wdT = [ar.al(D, BF16) for _ in range(8)]
    xd_t2 = [ar.al(D, BF16) for _ in range(2)]
    xq_fm2 = [ar.al(D, BF16) for _ in range(2)]
    ffn4_2 = [ar.al(D, F32) for _ in range(2)]
    z4_2 = [ar.al(D, F32), s_z[0:128, 0:D]]
    st4 = {k: ar.al(1, F32) for k in ("ssum", "ssq", "sm", "sv", "sr")}
    lnb4 = dict(sq=s_t[:, 0:D], ssum=st4["ssum"], ssq=st4["ssq"],
                sm=st4["sm"], sv=st4["sv"], sr=st4["sr"], eps=lnb["eps"])
    lnb4_2 = [lnb4, lnb]
    bias_bc = ar.al(D, F32)

    prog.dma("sync", bias_bc, b4_in.broadcast_to([128, D]), dsem)
    wc4 = prog.n(dsem)

    for oc in range(8):
        prog.dma("sync", wstg4, w4_in[128 * oc:128 * (oc + 1), :], wsem4,
                 waits=[(vsem, prog.n(vsem)), (ssem, prog.n(ssem))])
        wq_ = emit_quant(prog, sems, qb_, wstg4, wd_c,
                         waits0=[(wsem4, prog.n(wsem4))])
        for ic in range(8):
            tw = [(vsem, wq_), (dsem, wc4)]
            if oc * 8 + ic >= 2:
                tw.append((vsem, prog.n(vsem)))
            prog.emit("tensor", lambda h, ic=ic: h.transpose(
                pp[ic % 2][:, 0:128].bitcast(BF16)[:, 0:128],
                wd_c[:, 128 * ic:128 * (ic + 1)], s_idb[:]),
                sem=tsem, waits=tw)
            prog.emit("vector", lambda h, ic=ic, oc=oc: h.tensor_copy(
                wdT[ic][:, 128 * oc:128 * (oc + 1)],
                pp[ic % 2][:, 0:128].bitcast(BF16)[:, 0:128]),
                sem=vsem, waits=[(tsem, prog.n(tsem))])
    wwd = prog.n(vsem)

    xq_ready = {}
    resid_done = {}
    ffn_mm_done = {}
    ffn_evac_done = {}
    out_dma_done = {}
    for tt in range(16):
        par = tt % 2
        buf = s_x[par]
        xsem = xs4[par]
        xd_t = xd_t2[par]
        xq_fm = xq_fm2[par]
        ffn4 = ffn4_2[par]
        z4 = z4_2[par]
        qbp = qb2[par]
        lnp = lnb4_2[par]
        w = [(dsem, wc4)]
        if tt >= 2:
            w = [(vsem, resid_done[tt - 2])]
        prog.dma("sync", buf[:], x3_loc[128 * tt:128 * (tt + 1), :], xsem,
                 waits=w)
        wd = prog.n(xsem)
        emit_plain_ln(prog, sems, lnp, buf[:], z4, D,
                      extra_in_waits=[(xsem, wd),
                                      (tsem, ffn_mm_done.get(tt - 2, 0))])
        emit_quant(prog, sems, qbp, z4, xd_t,
                   waits0=[(vsem, prog.n(vsem))])
        wq_ = prog.n(vsem)
        for ic in range(8):
            tw = [(vsem, wq_), (vsem, prog.n(vsem))]
            prog.emit("tensor", lambda h, ic=ic, xd_t=xd_t: h.transpose(
                pp[ic % 2][:, 0:128].bitcast(BF16)[:, 0:128],
                xd_t[:, 128 * ic:128 * (ic + 1)], s_idb[:]),
                sem=tsem, waits=tw)
            prog.emit("vector", lambda h, ic=ic, xq_fm=xq_fm: h.tensor_copy(
                xq_fm[:, 128 * ic:128 * (ic + 1)],
                pp[ic % 2][:, 0:128].bitcast(BF16)[:, 0:128]),
                sem=vsem, waits=[(tsem, prog.n(tsem))])
        wxq = prog.n(vsem)
        xq_ready[tt] = wxq
        for jc in range(2):
            bank = pp[4 + 2 * (par) + jc - jc] if False else pp[4 + jc]
            for ic in range(8):
                prog.emit("tensor", lambda h, jc=jc, ic=ic, xq_fm=xq_fm:
                          h.matmul(
                              pp[4 + jc][:], xq_fm[:, 128 * ic:128 * (ic + 1)],
                              wdT[ic][:, 512 * jc:512 * (jc + 1)],
                              start=(ic == 0), stop=(ic == 7)),
                          sem=tsem,
                          waits=[(vsem, max(wxq, wwd)),
                                 (vsem, ffn_evac_done.get(tt - 1, 0))])
            prog.emit("vector", lambda h, jc=jc, ffn4=ffn4:
                      h.scalar_tensor_tensor(
                          ffn4[:, 512 * jc:512 * (jc + 1)], pp[4 + jc][:], 1.0,
                          bias_bc[:, 512 * jc:512 * (jc + 1)],
                          op0=OP.mult, op1=OP.add),
                      sem=vsem,
                      waits=[(tsem, prog.n(tsem)),
                             out_dma_done.get(tt - 2, (dsem, 0))])
        ffn_evac_done[tt] = prog.n(vsem)
        ffn_mm_done[tt] = prog.n(tsem)
        emit_plain_ln(prog, sems, lnp, ffn4, ffn4, D,
                      extra_in_waits=[(vsem, prog.n(vsem)),
                                      out_dma_done.get(tt - 2, (dsem, 0))])
        prog.emit("vector", lambda h, buf=buf, ffn4=ffn4: h.tensor_tensor(
            ffn4, ffn4, buf[:], op=OP.add),
            sem=vsem, waits=[out_dma_done.get(tt - 2, (dsem, 0))])
        resid_done[tt] = prog.n(vsem)
        prog.dma("sync", out_d[128 * tt:128 * (tt + 1), :], ffn4, os4[par],
                 waits=[(vsem, prog.n(vsem))])
        out_dma_done[tt] = (os4[par], prog.n(os4[par]))
    prog.wait("sync", (os4[0], prog.n(os4[0])), (os4[1], prog.n(os4[1])))


def build_lt_blocks_v2(th):
    S = 4096
    out = np.zeros((16, 3, 128, 128), dtype=np.float64)

    def lrow_val(q, k):
        d = q - k
        if abs(d) > 3 or k < 0 or k >= S:
            return 0.0
        num = np.exp(-(d * d) / (2.0 * 0.4 ** 2))
        den = 1e-10
        for j in range(max(0, q - 3), min(S - 1, q + 3) + 1):
            dd = q - j
            den += np.exp(-(dd * dd) / (2.0 * 0.4 ** 2))
        return num / den

    for qb in range(16):
        for s3 in range(3):
            gk = 16 * th + qb - 1 + s3
            if gk < 0 or gk > 31:
                continue
            for ql in range(128):
                q_true = th * 2048 + qb * 128 + ql
                for dk in range(-3, 4):
                    k_true = q_true + dk
                    kl = k_true - 128 * gk
                    if 0 <= kl < 128:
                        out[qb, s3, kl, ql] = 0.4 * lrow_val(q_true, k_true)
    return out


def prep_fused_inputs(inputs, core):
    b, th = core // 2, core % 2
    d1 = prep_launch1_inputs(inputs, core)
    f32 = np.float32
    x_res = np.ascontiguousarray(
        np.asarray(inputs["L0_attn_x"])[b, 2048 * th:2048 * (th + 1)]).astype(f32)
    p = np.arange(128)
    idx_y = np.zeros((128, 32), np.int32)
    for tt in range(16):
        for blk in range(2):
            idx_y[:, 2 * tt + blk] = blk * 4096 + 2048 * th + 128 * tt + p
    idx_vb = np.zeros((128, 24), np.int32)
    for qt in range(4):
        for s6 in range(6):
            gb = min(max(16 * th + 4 * qt - 1 + s6, 0), 31)
            idx_vb[:, 6 * qt + s6] = 128 * gb + p
    d3 = prep_launch3_inputs(inputs, core, [np.zeros((4096, 1024), f32)] * 4)
    lt = build_lt_blocks_v2(th)
    out = dict(d1)
    out.update(dict(
        x_res=x_res,
        lnab=np.stack([np.asarray(inputs["L0_attn_ln_a"]),
                       np.asarray(inputs["L0_attn_ln_b"])]).astype(f32),
        theta=np.asarray(inputs["L0_ffn_theta"]).astype(f32),
        thn=np.asarray(inputs["L0_ffn_theta_noise"]).astype(f32),
        phi=np.asarray(inputs["L0_ffn_phi"]).astype(f32),
        phn=np.asarray(inputs["L0_ffn_phi_noise"]).astype(f32),
        idx_y=idx_y,
        wk_l=d3["wk_l"], wv_l=d3["wv_l"], wq_l=d3["wq_l"], wo_l=d3["wo_l"],
        bk=d3["bk"], bq=d3["bq"], bvaug=d3["bvaug"], bo=d3["bo"],
        lt=np.ascontiguousarray(
            lt.transpose(2, 0, 1, 3).reshape(128, 16 * 3 * 128)).astype(f32),
        idx_vb=idx_vb,
        weight=np.asarray(inputs["L1_ffn_weight"]).astype(f32),
        bias=np.asarray(inputs["L1_ffn_bias"]).reshape(1, 1024).astype(f32),
    ))
    return out


from concourse.bass_utils import run_bass_kernel_spmd

_NCF = []


def kernel(**inputs):
    inputs = {k: np.asarray(v) for k, v in inputs.items()}
    if not _NCF:
        _NCF.append(build_fused())
    cores = list(range(8))
    res = run_bass_kernel_spmd(
        _NCF[0], [prep_fused_inputs(inputs, c) for c in cores], core_ids=cores)
    out = np.stack([np.concatenate([res.results[2 * b]["out_rows"],
                                    res.results[2 * b + 1]["out_rows"]], axis=0)
                    for b in range(4)])
    return out.astype(np.float32)
